# version 25
# baseline (speedup 1.0000x reference)
import numpy as np
import ml_dtypes
import concourse.bass as bass
import concourse.mybir as mybir
from concourse.bass_utils import run_bass_kernel_spmd

F32 = mybir.dt.float32
BF16 = mybir.dt.bfloat16
AF = mybir.ActivationFunctionType
ALU = mybir.AluOpType
AX = mybir.AxisListType.X
NPBF = ml_dtypes.bfloat16

DEBUG = False
D = 1024
H = 16
DK = 64
NE = 8
FF = 2048
T = 512
TS = 256
EPS = 1e-5

ENGS = ("pe", "act", "dve", "sp", "pool")
DMAQ = {"sp": ("dh0", "dh1"), "pool": ("ds0", "ds1")}
DMASEMS = ("dh0", "dh1", "ds0", "ds1")
CSEM = {"pe": "pe", "act": "act", "dve": "dve", "sp": "spc", "pool": "poolc"}
ALLSEMS = ("pe", "act", "dve", "spc", "poolc", "dh0", "dh1", "ds0", "ds1")


class Reg:
    __slots__ = ("w", "rs")

    def __init__(self):
        self.w = None
        self.rs = []

    def rd(self):
        return [self.w] if self.w is not None else []

    def wr(self):
        out = list(self.rs)
        if self.w is not None:
            out.append(self.w)
        return out

    def setw(self, tok):
        if tok is not None:
            self.w = tok
            self.rs = []

    def addr(self, tok):
        if tok is not None:
            self.rs.append(tok)


class Bld:
    def __init__(self):
        self.counts = {s: 0 for s in ALLSEMS}
        self.stream = {e: [] for e in ENGS}
        self.wm = {e: {s: 0 for s in ALLSEMS} for e in ENGS}
        self.dmaq_idx = {"sp": 0, "pool": 0}

    def emit(self, eng, fn, waits, inc):
        ws = {}
        for t in waits:
            if t is None:
                continue
            s, v = t
            if v > self.wm[eng][s] and v > ws.get(s, 0):
                ws[s] = v
        for s, v in ws.items():
            self.wm[eng][s] = v
        tok = None
        if inc is not None:
            n = 16 if inc in DMASEMS else 1
            self.counts[inc] += n
            tok = (inc, self.counts[inc])
        self.stream[eng].append((tuple(sorted(ws.items())), fn, inc))
        return tok

    def op(self, eng, fn, rd=(), wr=(), sem=0, extra=()):
        waits = list(extra)
        if sem == 0:
            if eng in DMAQ:
                i = self.dmaq_idx[eng]
                self.dmaq_idx[eng] = 1 - i
                sem = DMAQ[eng][i]
                waits.append((sem, self.counts[sem]))
            else:
                sem = CSEM[eng]
        for r in rd:
            waits += r.rd()
        for r in wr:
            waits += r.wr()
        tok = self.emit(eng, fn, waits, sem)
        for r in rd:
            r.addr(tok)
        for r in wr:
            r.setw(tok)
        return tok

    def mmg(self, mms, wr_reg, rd_each):
        n = len(mms)
        all_rd = []
        tok = None
        for i, (fn, rds) in enumerate(zip(mms, rd_each)):
            waits = list(wr_reg.wr()) if i == 0 else []
            for r in rds:
                waits += r.rd()
                all_rd.append(r)
            tok = self.emit("pe", fn, waits, "pe" if i == n - 1 else None)
        for r in all_rd:
            r.addr(tok)
        wr_reg.setw(tok)
        return tok

    def barrier(self):
        for e in ENGS:
            self.emit(e, lambda eng: eng.nop(),
                      [(CSEM[e], self.counts[CSEM[e]])], CSEM[e])
        snap = dict(self.counts)
        waits = [(s, snap[s]) for s in ALLSEMS]
        for e in ENGS:
            self.emit(e, None, waits, None)


def build(S):
    assert S % T == 0
    NT = S // T
    CC = 1024 if S % 1024 == 0 else S
    NCC = S // CC
    NTS = CC // TS
    NKT = S // 128

    nc = bass.Bass("TRN2", target_bir_lowering=False)

    xT = nc.dram_tensor("xT", [D, S], F32, kind="ExternalInput").ap()
    wqt = nc.dram_tensor("wqt", [D, D], F32, kind="ExternalInput").ap()
    wkt = nc.dram_tensor("wkt", [D, D], F32, kind="ExternalInput").ap()
    wvt = nc.dram_tensor("wvt", [D, D], F32, kind="ExternalInput").ap()
    wot = nc.dram_tensor("wot", [D, D], F32, kind="ExternalInput").ap()
    wg = nc.dram_tensor("wg", [D, NE], F32, kind="ExternalInput").ap()
    bgd = nc.dram_tensor("bgd", [1, NE], F32, kind="ExternalInput").ap()
    w1 = nc.dram_tensor("w1", [NE, D, FF], F32, kind="ExternalInput").ap()
    b1r = nc.dram_tensor("b1r", [128, 128], F32, kind="ExternalInput").ap()
    w2 = nc.dram_tensor("w2", [NE, FF, D], F32, kind="ExternalInput").ap()
    b2d = nc.dram_tensor("b2d", [NE, D], F32, kind="ExternalInput").ap()
    gamd = nc.dram_tensor("gamd", [128, 8], F32, kind="ExternalInput").ap()
    betd = nc.dram_tensor("betd", [128, 8], F32, kind="ExternalInput").ap()
    ctabd = nc.dram_tensor("ctabd", [128, S], F32, kind="ExternalInput").ap()
    stabd = nc.dram_tensor("stabd", [128, S], F32, kind="ExternalInput").ap()
    p2d = nc.dram_tensor("p2d", [128, 128], F32, kind="ExternalInput").ap()
    idd = nc.dram_tensor("idd", [128, 128], F32, kind="ExternalInput").ap()
    maskd = nc.dram_tensor("maskd", [128, 4 * T], F32, kind="ExternalInput").ap()
    onesd = nc.dram_tensor("onesd", [1, 128], F32, kind="ExternalInput").ap()
    onescd = nc.dram_tensor("onescd", [128, 1], F32, kind="ExternalInput").ap()
    epsd = nc.dram_tensor("epsd", [1, 1], F32, kind="ExternalInput").ap()
    onehd = nc.dram_tensor("onehd", [8, 8 * 128], F32, kind="ExternalInput").ap()
    out_d = nc.dram_tensor("out", [S, D], F32, kind="ExternalOutput").ap()

    if DEBUG:
        dbg_ln1 = nc.dram_tensor("dbg_ln1", [D, S], BF16, kind="Internal").ap()
        dbg_rows = nc.dram_tensor("dbg_rows", [S // T, 6 * T], F32, kind="Internal").ap()
    q_sp = nc.dram_tensor("q_sp", [D, S], BF16, kind="Internal").ap()
    k_sp = nc.dram_tensor("k_sp", [D, S], BF16, kind="Internal").ap()
    v_sp = nc.dram_tensor("v_sp", [NKT, 128, H * 65], BF16, kind="Internal").ap()
    ln2_sp = nc.dram_tensor("ln2_sp", [D, S], F32, kind="Internal").ap()
    y1_sp = nc.dram_tensor("y1_sp", [D, S], F32, kind="Internal").ap()
    wt_sp = nc.dram_tensor("wt_sp", [NE, S], F32, kind="Internal").ap()

    b = Bld()

    ones_r = nc.alloc_sbuf_tensor("ones_r", [1, 128], F32)
    ones_b = nc.alloc_sbuf_tensor("ones_b", [1, 128], BF16)
    ones_c = nc.alloc_sbuf_tensor("ones_c", [128, 1], F32)
    id_f = nc.alloc_sbuf_tensor("id_f", [128, 128], F32)
    rows = nc.alloc_sbuf_tensor("rows", [1, 5 * T], F32)
    epss = nc.alloc_sbuf_tensor("epss", [1, 1], F32)
    sqp = [nc.alloc_psum_tensor(f"sqp{i}", [128, 1024], F32) for i in range(2)]
    bank = [sqp[0].ap()[:, 0:512], sqp[0].ap()[:, 512:1024],
            sqp[1].ap()[:, 0:512], sqp[1].ap()[:, 512:1024]]
    bank += [nc.alloc_psum_tensor(f"bank{i}", [128, 512], F32) for i in range(4, 8)]
    sems = {s: nc.alloc_semaphore(s) for s in ALLSEMS}

    _fence = nc.alloc_sbuf_tensor("fence", [1, 8], F32)
    FREE0 = (nc.lookup_mloc(_fence).addr + 63) & ~31
    cur = [FREE0]

    def salloc(name, shape, dt):
        width = int(np.prod(shape[1:]))
        nbytes = (width * (2 if dt == BF16 else 4) + 31) & ~31
        off = cur[0]
        cur[0] += nbytes
        assert cur[0] <= 223 * 1024, f"SBUF overflow at {name}: {cur[0] // 1024} KiB"
        return nc.alloc_sbuf_tensor_at(name, list(shape), dt, offset=off)

    R_const = Reg()
    R_rows = Reg()
    R_bank = [Reg() for _ in range(8)]

    def emit_ln(get_x, x_regs, width, sqB, R_sqB, bs=2, ba=3):
        bk2, bk3 = bank[bs], bank[ba]
        x_ap = [get_x(dc) for dc in range(8)]
        mms = [(lambda xa, dc: lambda e: e.matmul(
            bk2[0:1, 0:width], ones_c.ap(), xa,
            start=(dc == 0), stop=(dc == 7)))(x_ap[dc], dc) for dc in range(8)]
        b.mmg(mms, R_bank[bs], [[R_const, x_regs[dc]] for dc in range(8)])
        for dc in range(8):
            sl = dc % 2
            b.op("dve", (lambda xa, sl: lambda e: e.tensor_mul(
                sqB[sl][:, 0:width], xa, xa))(x_ap[dc], sl),
                 rd=(x_regs[dc],), wr=(R_sqB[sl],))
            fn = (lambda dc, sl: lambda e: e.matmul(
                bk2[32:33, 0:width], ones_c.ap(), sqB[sl][:, 0:width],
                start=(dc == 0), stop=(dc == 7)))(dc, sl)
            w = (R_bank[bs].wr() if dc == 0 else []) + R_const.rd() + R_sqB[sl].rd()
            tok = b.emit("pe", fn, w, "pe")
            R_const.addr(tok)
            R_sqB[sl].addr(tok)
            if dc == 7:
                R_bank[bs].setw(tok)
        rM = rows[0:1, 0 * T:0 * T + width]
        rV = rows[0:1, 1 * T:1 * T + width]
        rA = rows[0:1, 2 * T:2 * T + width]
        rB = rows[0:1, 3 * T:3 * T + width]
        rT_ = rows[0:1, 4 * T:4 * T + width]
        rL = rT_
        b.op("dve", lambda e: e.tensor_scalar_mul(rM, bank[bs][0:1, 0:width], 1.0 / D),
             rd=(R_bank[bs],), wr=(R_rows,))
        b.op("dve", lambda e: e.tensor_scalar_mul(rV, bank[bs][32:33, 0:width], 1.0 / D),
             rd=(R_bank[bs],), wr=(R_rows,))
        b.op("dve", lambda e: e.tensor_mul(rT_, rM, rM), rd=(R_rows,), wr=(R_rows,))
        b.op("dve", lambda e: e.tensor_sub(rV, rV, rT_), rd=(R_rows,), wr=(R_rows,))
        b.op("act", lambda e: e.activation(rL, rV, AF.Ln, bias=epss[0:1, 0:1]),
             rd=(R_rows,), wr=(R_rows,))
        b.op("act", lambda e: e.activation(rA, rL, AF.Exp, scale=-0.5),
             rd=(R_rows,), wr=(R_rows,))
        b.op("dve", lambda e: e.scalar_tensor_tensor(rB, rM, -1.0, rA,
                                                     ALU.mult, ALU.mult),
             rd=(R_rows,), wr=(R_rows,))
        b.mmg([lambda e: e.matmul(bk3[:, 0:width], ones_r[0:1, 0:128], rA,
                                  start=True, stop=True)],
              R_bank[ba], [[R_const, R_rows]])
        b.mmg([lambda e: e.matmul(bk2[:, 0:width], ones_r[0:1, 0:128], rB,
                                  start=True, stop=True)],
              R_bank[bs], [[R_const, R_rows]])

    cs = salloc("cs_tab", [128, S], F32)
    ss = salloc("ss_tab", [128, S], F32)
    p2s = salloc("p2s", [128, 128], F32)
    wq_s = salloc("wq_s", [128, 8 * 1024], BF16)
    wk_s = salloc("wk_s", [128, 8 * 1024], BF16)
    wv_s = salloc("wv_s", [128, 8 * 1024], BF16)
    xin = [salloc(f"xin{i}", [128, 8 * T], F32) for i in range(2)]
    ln1 = salloc("ln1", [128, 8 * T], BF16)
    sq1 = [salloc(f"sq1_{i}", [128, T], F32) for i in range(2)]
    qraw = [salloc(f"qraw{i}", [128, T], F32) for i in range(2)]
    tm0 = [salloc(f"tm0{i}", [128, T], F32) for i in range(2)]
    tm1 = [salloc(f"tm1{i}", [128, T], F32) for i in range(2)]
    qst = salloc("qst", [128, 8 * T], BF16)
    kst = salloc("kst", [128, 8 * T], BF16)
    vfm = [salloc(f"vfm{i}", [128, T], F32) for i in range(2)]
    vst = salloc("vst", [128, 4 * 1040], BF16)

    R_xin = [Reg(), Reg()]
    R_ln1 = [Reg() for _ in range(8)]
    R_sq1 = [Reg(), Reg()]
    R_qraw = [Reg(), Reg()]
    R_tm0 = [Reg(), Reg()]
    R_tm1 = [Reg(), Reg()]
    R_qst = [Reg() for _ in range(8)]
    R_kst = [Reg() for _ in range(8)]
    R_vfm = [Reg(), Reg()]
    R_vst = [Reg() for _ in range(4)]
    R_qsp = [Reg() for _ in range(NT)]
    R_ksp = [Reg() for _ in range(NT)]
    R_vsp = [Reg() for _ in range(NT)]

    b.op("dve", lambda e: e.memset(vst[:, :], 1.0), wr=tuple(R_vst))

    b.op("pool", lambda e: e.dma_start(ones_b[:, :], onesd), wr=(R_const,))
    for dst, src in ((ones_r, onesd), (ones_c, onescd), (id_f, idd),
                     (cs, ctabd), (ss, stabd), (p2s, p2d), (epss, epsd)):
        b.op("sp", (lambda d, s: lambda e: e.dma_start(d[:, :], s))(dst, src),
             wr=(R_const,))
    for dst, src in ((wq_s, wqt), (wk_s, wkt), (wv_s, wvt)):
        b.op("pool", (lambda d, s: lambda e: e.dma_start(
            d.ap().rearrange("p (kc d) -> p kc d", kc=8),
            s.rearrange("(kc p) d -> p kc d", p=128)))(dst, src),
             wr=(R_const,))

    for tc in range(NT):
        xb = tc % 2
        b.op("sp", (lambda tc, xb: lambda e: e.dma_start(
            xin[xb].ap().rearrange("p (dc t) -> p dc t", dc=8),
            xT[:, tc * T:(tc + 1) * T].rearrange("(dc p) t -> p dc t", p=128)))(tc, xb),
             wr=(R_xin[xb],))
        emit_ln(lambda dc: xin[xb][:, dc * T:(dc + 1) * T], [R_xin[xb]] * 8,
                T, sq1, R_sq1)
        if DEBUG:
            b.op("sp", (lambda tc: lambda e: e.dma_start(
                dbg_rows[tc:tc + 1, :], rows[0:1, :]))(tc), rd=(R_rows,))
        for dc in range(8):
            sl = dc % 2
            b.op("dve", (lambda dc, sl, xb: lambda e: e.tensor_mul(
                tm0[sl][:, :], xin[xb][:, dc * T:(dc + 1) * T], bank[3][:, :]))(dc, sl, xb),
                 rd=(R_xin[xb], R_bank[3]), wr=(R_tm0[sl],))
            b.op("dve", (lambda dc, sl: lambda e: e.tensor_add(
                ln1[:, dc * T:(dc + 1) * T], tm0[sl][:, :], bank[2][:, :]))(dc, sl),
                 rd=(R_tm0[sl], R_bank[2]), wr=(R_ln1[dc],))

        def proj_group(wi, dc):
            wsb = (wq_s, wk_s, wv_s)[wi]
            g = (wi * 8 + dc) % 2
            mms = [(lambda kc, dc, wsb, g: lambda e: e.matmul(
                bank[g][:, :],
                wsb[:, kc * 1024 + dc * 128:kc * 1024 + (dc + 1) * 128],
                ln1[:, kc * T:(kc + 1) * T],
                start=(kc == 0), stop=(kc == 7)))(kc, dc, wsb, g)
                for kc in range(8)]
            b.mmg(mms, R_bank[g], [[R_const, R_ln1[kc]] for kc in range(8)])

        def proj_post(wi, dc, tc):
            g = (wi * 8 + dc) % 2
            if wi < 2:
                qs = dc % 2
                b.op("act", (lambda qs, g: lambda e: e.activation(
                    qraw[qs][:, :], bank[g][:, :], AF.Identity))(qs, g),
                     rd=(R_bank[g],), wr=(R_qraw[qs],))
                sb_i = 5 + (dc % 2)
                b.mmg([(lambda qs, sb_i: lambda e: e.matmul(
                    bank[sb_i][:, :], p2s[:, :], qraw[qs][:, :],
                    start=True, stop=True))(qs, sb_i)],
                      R_bank[sb_i], [[R_const, R_qraw[qs]]])
                b.op("dve", (lambda qs, tc: lambda e: e.tensor_mul(
                    tm0[qs][:, :], qraw[qs][:, :], cs[:, tc * T:(tc + 1) * T]))(qs, tc),
                     rd=(R_qraw[qs], R_const), wr=(R_tm0[qs],))
                b.op("dve", (lambda qs, sb_i, tc: lambda e: e.tensor_mul(
                    tm1[qs][:, :], bank[sb_i][:, :], ss[:, tc * T:(tc + 1) * T]))(qs, sb_i, tc),
                     rd=(R_bank[sb_i], R_const), wr=(R_tm1[qs],))
                stage = qst if wi == 0 else kst
                sreg = (R_qst if wi == 0 else R_kst)[dc]
                b.op("dve", (lambda qs, dc, stage: lambda e: e.tensor_add(
                    stage[:, dc * T:(dc + 1) * T], tm0[qs][:, :], tm1[qs][:, :]))(qs, dc, stage),
                     rd=(R_tm0[qs], R_tm1[qs]), wr=(sreg,))
            else:
                vs = dc % 2
                b.op("act", (lambda vs, g: lambda e: e.activation(
                    vfm[vs][:, :], bank[g][:, :], AF.Identity))(vs, g),
                     rd=(R_bank[g],), wr=(R_vfm[vs],))
                for tt in range(4):
                    tb = 4 if (dc * 4 + tt) % 2 == 0 else 7
                    b.mmg([(lambda vs, tt, tb: lambda e: e.transpose(
                        bank[tb][:, 0:128],
                        vfm[vs][:, tt * 128:(tt + 1) * 128], id_f[:, :]))(vs, tt, tb)],
                          R_bank[tb], [[R_vfm[vs], R_const]])
                    for hs2 in range(2):
                        hcol = 65 * (2 * dc + hs2)
                        b.op("dve", (lambda tt, tb, hs2, hcol: lambda e: e.tensor_copy(
                            vst[:, tt * 1040 + hcol:tt * 1040 + hcol + 64],
                            bank[tb][:, hs2 * 64:hs2 * 64 + 64]))(tt, tb, hs2, hcol),
                             rd=(R_bank[tb],), wr=(R_vst[tt],))

        items = [(wi, dc) for wi in range(3) for dc in range(8)]
        proj_group(*items[0])
        for i in range(24):
            if i + 1 < 24:
                proj_group(*items[i + 1])
            proj_post(*items[i], tc)
        b.op("sp", (lambda tc: lambda e: e.dma_start(
            q_sp[:, tc * T:(tc + 1) * T].rearrange("(dc p) t -> p dc t", p=128),
            qst.ap().rearrange("p (dc t) -> p dc t", dc=8)))(tc),
             rd=tuple(R_qst), wr=(R_qsp[tc],))
        b.op("sp", (lambda tc: lambda e: e.dma_start(
            k_sp[:, tc * T:(tc + 1) * T].rearrange("(dc p) t -> p dc t", p=128),
            kst.ap().rearrange("p (dc t) -> p dc t", dc=8)))(tc),
             rd=tuple(R_kst), wr=(R_ksp[tc],))
        b.op("sp", (lambda tc: lambda e: e.dma_start(
            v_sp[tc * 4:(tc + 1) * 4].rearrange("tt p c -> p tt c"),
            vst.ap().rearrange("p (tt c) -> p tt c", tt=4)))(tc),
             rd=tuple(R_vst), wr=(R_vsp[tc],))

    b.barrier()
    cur[0] = FREE0
    kt_s = salloc("kt_s", [128, 8 * S], BF16)
    vh_s = salloc("vh_s", [128, NKT * 1040], BF16)
    qt_s = salloc("qt_s", [128, 8 * T], BF16)
    wo_dc = [salloc(f"wo_dc{i}", [128, 8 * 128], BF16) for i in range(2)]
    mk_s = salloc("mk_s", [128, 4 * T], BF16)
    pr2 = [salloc(f"pr{i}", [128, 2 * T], BF16) for i in range(2)]
    _xy1 = salloc("xy1", [128, 8 * T], F32)
    xy1b = [_xy1, _xy1]
    att_b = [salloc(f"att_s{i}", [128, 8 * T], BF16) for i in range(2)]
    rb_s = salloc("rb_s", [128, T], F32)
    rr_s = salloc("rr_s", [1, 2 * T], F32)
    rr_b = salloc("rr_b", [1, 2 * T], BF16)
    _sq2 = salloc("sq2", [128, T], F32)
    sq2 = [_sq2, _sq2]
    wgs = salloc("wgs", [128, 8 * 8], F32)
    bgs = salloc("bgs", [1, 8], F32)
    rt = salloc("rt", [128, 6 * 8], F32)
    rt1 = salloc("rt1", [128, 8], F32)
    wst = salloc("wst", [8, T], F32)

    R_kts = [Reg() for _ in range(8)]
    R_vhs = [Reg() for _ in range(NKT)]
    R_vones = Reg()
    R_qts = Reg()
    R_wodc = [Reg(), Reg()]
    R_mks = Reg()
    R_pr2 = [Reg() for _ in range(2)]
    _R_xy1 = [Reg() for _ in range(8)]
    R_xy1b = [_R_xy1, _R_xy1]
    R_attb = [[Reg() for _ in range(16)] for _ in range(2)]
    R_rbs = Reg()
    R_rr1 = Reg()
    R_rrs = [R_rr1, R_rr1]
    R_rrx = Reg()
    R_attf = [Reg(), Reg()]
    _R_sq2 = Reg()
    R_sq2 = [_R_sq2, _R_sq2]
    R_wgs = Reg()
    R_rt = [Reg() for _ in range(6)]
    R_rt1 = Reg()
    R_wst = [Reg() for _ in range(4)]
    R_l2sp = [[Reg() for _ in range(8)] for _ in range(NT)]
    R_y1sp = [[Reg() for _ in range(8)] for _ in range(NT)]
    R_wtsp = [Reg() for _ in range(NT)]

    for hp in range(8):
        b.op("sp", (lambda hp: lambda e: e.dma_start(
            kt_s[:, hp * S:(hp + 1) * S], k_sp[hp * 128:(hp + 1) * 128, :]))(hp),
             rd=tuple(R_ksp), wr=(R_kts[hp],))
    ng = (NKT + 7) // 8
    for g in range(ng):
        n = min(8, NKT - g * 8)
        b.op("pool", (lambda g, n: lambda e: e.dma_start(
            vh_s[:, g * 8 * 1040:(g * 8 + n) * 1040].rearrange(
                "p (kt c) -> p kt c", kt=n),
            v_sp[g * 8:g * 8 + n].rearrange("kt p c -> p kt c")))(g, n),
             rd=tuple(R_vsp), wr=tuple(R_vhs[g * 8:g * 8 + n]))
    b.op("pool", lambda e: e.dma_start(mk_s[:, :], maskd), wr=(R_mks,))
    b.op("sp", lambda e: e.dma_start(
        wgs.ap().rearrange("p (dc e8) -> p dc e8", dc=8),
        wg.rearrange("(dc p) e8 -> p dc e8", p=128)), wr=(R_wgs,))
    b.op("sp", lambda e: e.dma_start(bgs[:, :], bgd), wr=(R_wgs,))

    def make_tail(qc, par):
        xy1 = xy1b[par]
        att_s = att_b[par]
        R_xy1 = R_xy1b[par]
        R_att = R_attb[par]
        t = []

        def wo_load(dc):
            b.op("pool", (lambda dc: lambda e: e.dma_start(
                wo_dc[dc % 2].ap().rearrange("p (kc d) -> p kc d", kc=8),
                wot[:, dc * 128:(dc + 1) * 128].rearrange(
                    "(kc p) d -> p kc d", p=128)))(dc),
                 wr=(R_wodc[dc % 2],))

        def x_load():
            b.op("sp", (lambda qc: lambda e: e.dma_start(
                xy1.ap().rearrange("p (dc t) -> p dc t", dc=8),
                xT[:, qc * T:(qc + 1) * T].rearrange("(dc p) t -> p dc t", p=128)))(qc),
                 wr=tuple(R_xy1))
        t.append(lambda: (x_load(), wo_load(0), wo_load(1)))

        def wo_dc_step(dc):
            g = 6 + dc % 2
            mms = [(lambda kc, dc, g: lambda e: e.matmul(
                bank[g][:, :], wo_dc[dc % 2][:, kc * 128:(kc + 1) * 128],
                att_s[:, kc * T:(kc + 1) * T],
                start=(kc == 0), stop=(kc == 7)))(kc, dc, g) for kc in range(8)]
            b.mmg(mms, R_bank[g],
                  [[R_wodc[dc % 2], R_att[2 * kc], R_att[2 * kc + 1]] for kc in range(8)])
            if dc + 2 < 8:
                wo_load(dc + 2)
            b.op("dve", (lambda dc, g: lambda e: e.tensor_add(
                xy1[:, dc * T:(dc + 1) * T], bank[g][:, :],
                xy1[:, dc * T:(dc + 1) * T]))(dc, g),
                 rd=(R_bank[g],), wr=(R_xy1[dc],))
            b.op("sp", (lambda dc, qc: lambda e: e.dma_start(
                y1_sp[dc * 128:(dc + 1) * 128, qc * T:(qc + 1) * T],
                xy1[:, dc * T:(dc + 1) * T]))(dc, qc),
                 rd=(R_xy1[dc],), wr=(R_y1sp[qc][dc],))

        for dc in range(8):
            t.append((lambda dc: lambda: wo_dc_step(dc))(dc))

        def ln2_all():
            emit_ln(lambda dc: xy1[:, dc * T:(dc + 1) * T],
                    R_xy1, T, sq2, R_sq2, bs=6, ba=7)
            for dc in range(8):
                ln2_apply(dc)

        def ln2_apply(dc):
            b.op("dve", (lambda dc: lambda e: e.tensor_mul(
                xy1[:, dc * T:(dc + 1) * T], xy1[:, dc * T:(dc + 1) * T],
                bank[7][:, :]))(dc),
                 rd=(R_bank[7],), wr=(R_xy1[dc],),
                 extra=R_y1sp[qc][dc].rd())
            b.op("dve", (lambda dc: lambda e: e.tensor_add(
                xy1[:, dc * T:(dc + 1) * T], xy1[:, dc * T:(dc + 1) * T],
                bank[6][:, :]))(dc),
                 rd=(R_bank[6],), wr=(R_xy1[dc],))
            b.op("sp", (lambda dc, qc: lambda e: e.dma_start(
                ln2_sp[dc * 128:(dc + 1) * 128, qc * T:(qc + 1) * T],
                xy1[:, dc * T:(dc + 1) * T]))(dc, qc),
                 rd=(R_xy1[dc],), wr=(R_l2sp[qc][dc],))

        t.append(ln2_all)

        def router_tt(tt):
            mms = [(lambda dc, tt: lambda e: e.matmul(
                bank[6][0:128, 0:8],
                xy1[:, dc * T + tt * 128:dc * T + (tt + 1) * 128],
                wgs[:, dc * 8:(dc + 1) * 8],
                start=(dc == 0), stop=False))(dc, tt) for dc in range(8)]
            mms.append(lambda e: e.matmul(
                bank[6][0:128, 0:8], ones_r[0:1, 0:128], bgs[:, :],
                start=False, stop=True))
            b.mmg(mms, R_bank[6],
                  [[R_xy1[dc], R_wgs] for dc in range(8)] + [[R_const, R_wgs]])
            c = lambda i: rt[:, i * 8:(i + 1) * 8]
            v1c, v2c = rt1[:, 0:1], rt1[:, 1:2]
            w2c, w1c, tc1 = rt1[:, 2:3], rt1[:, 3:4], rt1[:, 4:5]
            b.op("dve", lambda e: e.reduce_max(v1c, bank[6][0:128, 0:8], axis=AX),
                 rd=(R_bank[6],), wr=(R_rt1,))
            b.op("dve", lambda e: e.tensor_scalar(c(0), bank[6][0:128, 0:8],
                                                  v1c, None, ALU.subtract),
                 rd=(R_bank[6], R_rt1), wr=(R_rt[0],))
            b.op("dve", lambda e: e.tensor_scalar(c(1), c(0), 0.0, None, ALU.is_equal),
                 rd=(R_rt[0],), wr=(R_rt[1],))
            b.op("dve", lambda e: e.scalar_tensor_tensor(
                c(2), c(1), -1e9, c(0), ALU.mult, ALU.add),
                 rd=(R_rt[0], R_rt[1]), wr=(R_rt[2],))
            b.op("dve", lambda e: e.reduce_max(v2c, c(2), axis=AX),
                 rd=(R_rt[2],), wr=(R_rt1,))
            b.op("act", lambda e: e.activation(tc1, v2c, AF.Exp, scale=-1.0),
                 rd=(R_rt1,), wr=(R_rt1,))
            b.op("dve", lambda e: e.tensor_scalar_add(tc1, tc1, 1.0),
                 rd=(R_rt1,), wr=(R_rt1,))
            b.op("dve", lambda e: e.reciprocal(w2c, tc1), rd=(R_rt1,), wr=(R_rt1,))
            b.op("dve", lambda e: e.tensor_scalar(w1c, w2c, -1.0, 1.0,
                                                  ALU.mult, ALU.add),
                 rd=(R_rt1,), wr=(R_rt1,))
            b.op("dve", lambda e: e.tensor_scalar(c(3), c(0), v2c, None, ALU.is_equal),
                 rd=(R_rt[0], R_rt1), wr=(R_rt[3],))
            b.op("dve", lambda e: e.tensor_scalar(c(4), c(3), w2c, None, ALU.mult),
                 rd=(R_rt[3], R_rt1), wr=(R_rt[4],))
            b.op("dve", lambda e: e.scalar_tensor_tensor(
                c(5), c(1), w1c, c(4), ALU.mult, ALU.add),
                 rd=(R_rt[1], R_rt1, R_rt[4]), wr=(R_rt[5],))
        def router_tp(tt):
            c = lambda i: rt[:, i * 8:(i + 1) * 8]
            b.mmg([(lambda: lambda e: e.transpose(
                bank[7][0:8, 0:128], c(5), id_f[:, :]))()],
                  R_bank[7], [[R_rt[5], R_const]])
            b.op("dve", (lambda tt: lambda e: e.tensor_copy(
                wst[:, tt * 128:(tt + 1) * 128], bank[7][0:8, 0:128]))(tt),
                 rd=(R_bank[7],), wr=(R_wst[tt],))

        for tt in range(4):
            t.append((lambda tt: lambda: router_tt(tt))(tt))
            t.append((lambda tt: lambda: router_tp(tt))(tt))
        t.append(lambda: b.op("sp", (lambda qc: lambda e: e.dma_start(
            wt_sp[:, qc * T:(qc + 1) * T], wst[:, :]))(qc),
            rd=tuple(R_wst), wr=(R_wtsp[qc],)))
        return t

    pend = []

    def drip(k):
        for _ in range(min(k, len(pend))):
            pend.pop(0)()

    for qc in range(NT):
        nk = 4 * (qc + 1)
        par = qc % 2
        att_s = att_b[par]
        R_att = R_attb[par]
        b.op("sp", (lambda qc: lambda e: e.dma_start(
            qt_s.ap().rearrange("p (hp t) -> p hp t", hp=8),
            q_sp[:, qc * T:(qc + 1) * T].rearrange("(hp p) t -> p hp t", p=128)))(qc),
             rd=(R_qsp[qc],), wr=(R_qts,))
        pending_norm = []
        for hp in range(8):
            def s_mm(kt, hs):
                bi = (2 * kt + hs) % 4
                fn = (lambda kt, hs, bi, hp: lambda e: e.matmul(
                    bank[bi][:, :],
                    kt_s[hs * 64:hs * 64 + 64,
                         hp * S + kt * 128:hp * S + (kt + 1) * 128],
                    qt_s[hs * 64:hs * 64 + 64, hp * T:(hp + 1) * T],
                    start=True, stop=True,
                    tile_position=(hs * 64, 0)))(kt, hs, bi, hp)
                b.mmg([fn], R_bank[bi], [[R_kts[hp], R_qts]])

            s_mm(0, 0)
            s_mm(0, 1)
            for kt in range(nk):
                if kt + 1 < nk:
                    s_mm(kt + 1, 0)
                    s_mm(kt + 1, 1)
                if kt == 1 or nk == 1:
                    for nf in pending_norm:
                        nf()
                    pending_norm = []
                if (kt == 2 and hp >= 1) or (kt == 0 and nk <= 2 and hp >= 1):
                    drip(5)
                slot = kt % 2
                b.op("act", (lambda slot: lambda e: e.activation(
                    pr2[slot][:, :], sqp[slot].ap(), AF.Exp, scale=0.125))(slot),
                     rd=(R_bank[(2 * kt) % 4], R_bank[(2 * kt + 1) % 4]),
                     wr=(R_pr2[slot],))
                if kt >= 4 * qc:
                    r = kt - 4 * qc
                    for hs in range(2):
                        b.op("dve", (lambda r, slot, hs: lambda e: e.tensor_mul(
                            pr2[slot][:, hs * T:(hs + 1) * T],
                            pr2[slot][:, hs * T:(hs + 1) * T],
                            mk_s[:, r * T:(r + 1) * T]))(r, slot, hs),
                             rd=(R_mks,), wr=(R_pr2[slot],))
                for hs in range(2):
                    h = 2 * hp + hs
                    pa = bank[4 + hs]
                    rpa = R_bank[4 + hs]
                    fn = (lambda kt, slot, h, pa, nk, hs: lambda e: e.matmul(
                        pa[0:65, :],
                        vh_s[:, kt * 1040 + 65 * h:kt * 1040 + 65 * h + 65],
                        pr2[slot][:, hs * T:(hs + 1) * T],
                        start=(kt == 0), stop=(kt == nk - 1)))(kt, slot, h, pa, nk, hs)
                    w = list(rpa.wr()) if kt == 0 else []
                    w += R_vhs[kt].rd() + R_vones.rd() + R_pr2[slot].rd()
                    tok = b.emit("pe", fn, w, "pe")
                    R_vhs[kt].addr(tok)
                    R_vones.addr(tok)
                    R_pr2[slot].addr(tok)
                    if kt == nk - 1:
                        rpa.setw(tok)
            for hs in range(2):
                pa = bank[4 + hs]
                rpa = R_bank[4 + hs]
                rr_l = rr_s[0:1, hs * T:(hs + 1) * T]
                aslice = att_s[hs * 64:hs * 64 + 64, hp * T:(hp + 1) * T]
                areg = R_att[2 * hp + hs]
                rr = rr_b[0:1, hs * T:(hs + 1) * T]
                b.op("act", (lambda rr_l, pa: lambda e: e.activation(
                    rr_l, pa[64:65, :], AF.Ln))(rr_l, pa),
                     rd=(rpa,), wr=(R_rrs[hs], R_attf[hs]))
                b.op("dve", (lambda aslice, pa: lambda e: e.tensor_copy(
                    aslice, pa[0:64, :]))(aslice, pa),
                     rd=(rpa, R_attf[hs]), wr=(areg,))
                b.op("act", (lambda rr, rr_l: lambda e: e.activation(
                    rr, rr_l, AF.Exp, scale=-1.0))(rr, rr_l),
                     rd=(R_rrs[hs],), wr=(R_rrx,))

                def norm_fin(hs, rr, aslice, areg):
                    rbb = 6 + hs
                    p0 = hs * 64
                    b.mmg([(lambda rr, rbb, p0: lambda e: e.matmul(
                        bank[rbb][p0:p0 + 64, :], ones_b[0:1, 0:64], rr,
                        start=True, stop=True,
                        tile_position=(0, p0)))(rr, rbb, p0)],
                          R_bank[rbb], [[R_const, R_rrx]])
                    b.op("dve", (lambda rbb, p0: lambda e: e.tensor_copy(
                        rb_s[p0:p0 + 64, :], bank[rbb][p0:p0 + 64, :]))(rbb, p0),
                         rd=(R_bank[rbb],), wr=(R_rbs,))
                    b.op("dve", (lambda aslice, p0: lambda e: e.tensor_mul(
                        aslice, aslice, rb_s[p0:p0 + 64, :]))(aslice, p0),
                         rd=(R_rbs,), wr=(areg,))

                pending_norm.append(
                    (lambda hs, rr, aslice, areg: lambda: norm_fin(
                        hs, rr, aslice, areg))(hs, rr, aslice, areg))
        for nf in pending_norm:
            nf()
        drip(len(pend))
        pend = make_tail(qc, par)
    drip(len(pend))

    b.barrier()
    cur[0] = FREE0
    w1h = [salloc(f"w1h{i}", [128, 8 * 1024], BF16) for i in range(2)]
    w2h = [salloc(f"w2h{i}", [128, 8 * 1024], BF16) for i in range(2)]
    ln2c = salloc("ln2c", [128, 8 * CC], BF16)
    acc = salloc("acc", [128, 8 * CC], F32)
    h_s = [salloc(f"h_s{i}", [128, TS], BF16) for i in range(2)]
    hs_s = [salloc(f"hs_s{i}", [128, TS], BF16) for i in range(2)]
    wb_s = [salloc(f"wb_s{i}", [128, TS], BF16) for i in range(4)]
    wrow8 = salloc("wrow8", [8, CC], F32)
    wrow8b = salloc("wrow8b", [8, CC], BF16)
    oneh_b = salloc("oneh_b", [8, 8 * 128], BF16)
    b1s = salloc("b1s", [128, 128], F32)
    b2s = salloc("b2s", [8, 1024], F32)
    gms = salloc("gms", [128, 8], F32)
    bts = salloc("bts", [128, 8], F32)
    _l2f = salloc("l2f", [128, 8 * TS], F32)
    _y1f = salloc("y1f", [128, 8 * TS], F32)
    l2f = [_l2f, _l2f]
    y1f = [_y1f, _y1f]
    y2wb = [salloc(f"y2w{i}", [128, 8 * TS], F32) for i in range(4)]
    tb0 = [salloc(f"tb0{i}", [128, TS], F32) for i in range(2)]
    tb1 = [salloc(f"tb1{i}", [128, TS], F32) for i in range(2)]
    otk = [salloc(f"otk{i}", [128, 1024], F32) for i in range(2)]
    sq3 = [salloc(f"sq3_{i}", [128, TS], F32) for i in range(2)]

    R_w1h = [Reg(), Reg()]
    R_w2h = [Reg(), Reg()]
    R_ln2c = Reg()
    R_acc = [Reg() for _ in range(8)]
    R_h = [Reg(), Reg()]
    R_hs = [Reg(), Reg()]
    R_wb = [Reg() for _ in range(4)]
    R_wrow8 = Reg()
    R_wrow8b = Reg()
    R_cst3 = Reg()
    _R_l2f = Reg()
    _R_y1f = Reg()
    R_l2f = [_R_l2f, _R_l2f]
    R_y1f = [_R_y1f, _R_y1f]
    R_y2wb = [[Reg() for _ in range(8)] for _ in range(4)]
    R_tb0 = [Reg(), Reg()]
    R_tb1 = [Reg(), Reg()]
    R_otk = [Reg(), Reg()]
    R_sq3 = [Reg(), Reg()]

    b.op("sp", lambda e: e.dma_start(b1s[:, :], b1r), wr=(R_cst3,))
    b.op("pool", lambda e: e.dma_start(oneh_b[:, :], onehd), wr=(R_cst3,))
    b.op("sp", lambda e: e.dma_start(b2s[:, :], b2d), wr=(R_cst3,))
    b.op("sp", lambda e: e.dma_start(gms[:, :], gamd), wr=(R_cst3,))
    b.op("sp", lambda e: e.dma_start(bts[:, :], betd), wr=(R_cst3,))

    NP = NE * 2
    all_l2sp = [R_l2sp[i][dc] for i in range(NT) for dc in range(8)]
    all_wtsp = list(R_wtsp)

    def w_load(p, buf):
        e_, fh = p // 2, p % 2
        b.op("pool", (lambda e_, fh, buf: lambda eng: eng.dma_start(
            w1h[buf].ap().rearrange("p (dc f) -> p dc f", dc=8),
            w1[e_, :, fh * 1024:(fh + 1) * 1024].rearrange(
                "(dc p) f -> p dc f", p=128)))(e_, fh, buf),
             wr=(R_w1h[buf],))
        b.op("pool", (lambda e_, fh, buf: lambda eng: eng.dma_start(
            w2h[buf].ap().rearrange("p (fc d) -> p fc d", fc=8),
            w2[e_, fh * 1024:(fh + 1) * 1024, :].rearrange(
                "(fc p) d -> p fc d", p=128)))(e_, fh, buf),
             wr=(R_w2h[buf],))

    w_load(0, 0)
    pend3 = []
    for cc in range(NCC):
        b.op("pool", (lambda cc: lambda e: e.dma_start(
            ln2c.ap().rearrange("p (dc t) -> p dc t", dc=8),
            ln2_sp[:, cc * CC:(cc + 1) * CC].rearrange(
                "(dc p) t -> p dc t", p=128)))(cc),
             rd=tuple(all_l2sp), wr=(R_ln2c,))
        b.op("sp", (lambda cc: lambda e: e.dma_start(
            wrow8[:, :], wt_sp[:, cc * CC:(cc + 1) * CC]))(cc),
             rd=tuple(all_wtsp), wr=(R_wrow8,))
        b.op("dve", lambda e: e.tensor_copy(wrow8b[:, :], wrow8[:, :]),
             rd=(R_wrow8,), wr=(R_wrow8b,))
        for p in range(NP):
            e_, fh = p // 2, p % 2
            buf = p % 2
            pass
            if p + 1 < NP:
                w_load(p + 1, (p + 1) % 2)
            elif cc + 1 < NCC:
                w_load(0, (p + 1) % 2)
            def wb_build(ts):
                if fh != 0:
                    return
                wbs = ts
                b.mmg([(lambda e_, ts: lambda e: e.matmul(
                    bank[6][:, (ts % 2) * 256:(ts % 2) * 256 + TS],
                    oneh_b[:, e_ * 128:(e_ + 1) * 128],
                    wrow8b[:, ts * TS:(ts + 1) * TS],
                    start=True, stop=True))(e_, ts)],
                      R_bank[6], [[R_cst3, R_wrow8b]])
                b.op("act", (lambda ts, wbs: lambda e: e.activation(
                    wb_s[wbs][:, :], bank[6][:, (ts % 2) * 256:(ts % 2) * 256 + TS],
                    AF.Identity))(ts, wbs),
                     rd=(R_bank[6],), wr=(R_wb[wbs],))

            def w1_stage(ts, fc):
                hb = (ts * 8 + fc) % 2
                mms = [(lambda dc, fc, buf, ts, hb: lambda e: e.matmul(
                    bank[4 + hb][:, 0:TS],
                    w1h[buf][:, dc * 1024 + fc * 128:dc * 1024 + (fc + 1) * 128],
                    ln2c[:, dc * CC + ts * TS:dc * CC + (ts + 1) * TS],
                    start=(dc == 0), stop=(dc == 7)))(dc, fc, buf, ts, hb)
                    for dc in range(8)]
                b.mmg(mms, R_bank[4 + hb],
                      [[R_w1h[buf], R_ln2c] for _ in range(8)])
                fcg = fh * 8 + fc
                b.op("act", (lambda hb, e_, fcg: lambda e: e.activation(
                    h_s[hb][:, :], bank[4 + hb][:, 0:TS], AF.Relu,
                    bias=b1s[:, e_ * 16 + fcg:e_ * 16 + fcg + 1]))(hb, e_, fcg),
                     rd=(R_bank[4 + hb], R_cst3), wr=(R_h[hb],))
                b.op("dve", (lambda hb, ts: lambda e: e.tensor_mul(
                    hs_s[hb][:, :], h_s[hb][:, :], wb_s[ts][:, :]))(hb, ts),
                     rd=(R_h[hb], R_wb[ts]), wr=(R_hs[hb],))

            def w2_stage(ts, fc):
                hb = (ts * 8 + fc) % 2
                eo_open = eo_open_by[ts]
                is_last_piece = (p == NP - 1)
                grp_start = (fc == 0)
                grp_end = (fc == 7)
                tok = None
                for dc in range(8):
                    bi, half = dc // 2, dc % 2
                    stop = grp_end and half == 1 and not is_last_piece
                    fn = (lambda dc, fc, buf, hb, bi, half, stop, st0, ts: lambda e: e.matmul(
                        bank[bi][:, half * 256:half * 256 + TS],
                        w2h[buf][:, fc * 1024 + dc * 128:fc * 1024 + (dc + 1) * 128],
                        hs_s[hb][:, :],
                        start=st0, stop=stop))(dc, fc, buf, hb, bi, half, stop,
                                               grp_start and half == 0, ts)
                    w = []
                    if grp_start and not eo_open[bi]:
                        w += R_bank[bi].wr()
                        eo_open[bi] = True
                    w += R_w2h[buf].rd() + R_hs[hb].rd()
                    inc = "pe" if dc == 7 else None
                    tok = b.emit("pe", fn, w, inc)
                R_w2h[buf].addr(tok)
                R_hs[hb].addr(tok)
                if grp_end and not is_last_piece:
                    for bi in range(4):
                        R_bank[bi].setw(tok)

            def eo_evac(ts):
                if p == NP - 1:
                    mms = []
                    for dc in range(8):
                        bi, half = dc // 2, dc % 2
                        mms.append((lambda dc, ts, bi, half: lambda e: e.matmul(
                            bank[bi][:, half * 256:half * 256 + TS],
                            b2s[:, dc * 128:(dc + 1) * 128],
                            wrow8[:, ts * TS:(ts + 1) * TS],
                            start=False, stop=(half == 1)))(dc, ts, bi, half))
                    tok = None
                    for i, fn in enumerate(mms):
                        w = R_cst3.rd() + R_wrow8.rd()
                        tok = b.emit("pe", fn, w, "pe" if i == 7 else None)
                    R_cst3.addr(tok)
                    R_wrow8.addr(tok)
                    for bi in range(4):
                        R_bank[bi].setw(tok)
                for bi in range(4):
                    src_ap = bass.AP(sqp[bi // 2], (bi % 2) * 512,
                                     [[1024, 128], [256, 2], [1, TS]])
                    dst_ap = bass.AP(acc, 2 * bi * CC + ts * TS,
                                     [[8 * CC, 128], [CC, 2], [1, TS]])
                    if p == 0:
                        fn = (lambda d, s: lambda e: e.tensor_copy(d, s))(dst_ap, src_ap)
                    else:
                        fn = (lambda d, s: lambda e: e.tensor_add(d, s, d))(dst_ap, src_ap)
                    b.op("dve", fn, rd=(R_bank[bi],),
                         wr=(R_acc[2 * bi], R_acc[2 * bi + 1]))

            eo_open_by = {ts: [False] * 4 for ts in range(NTS)}
            stages = [(ts, fc) for ts in range(NTS) for fc in range(8)]
            wb_build(0)
            w1_stage(*stages[0])
            for i in range(len(stages)):
                nts_, nfc_ = stages[i + 1] if i + 1 < len(stages) else (None, None)
                if nts_ is not None:
                    if nfc_ == 0:
                        wb_build(nts_)
                    w1_stage(nts_, nfc_)
                w2_stage(*stages[i])
                if stages[i][1] == 7:
                    eo_evac(stages[i][0])

        def ln3_a(cc, ts):
            tsb = ts % 2
            t0 = cc * CC + ts * TS
            y2w = y2wb[ts]
            R_y2w = R_y2wb[ts]
            b.op("sp", (lambda t0, tsb: lambda e: e.dma_start(
                l2f[tsb].ap().rearrange("p (dc t) -> p dc t", dc=8),
                ln2_sp[:, t0:t0 + TS].rearrange("(dc p) t -> p dc t", p=128)))(t0, tsb),
                 rd=tuple(all_l2sp), wr=(R_l2f[tsb],))
            for dc in range(8):
                b.op("dve", (lambda dc, ts, tsb, cc, y2w: lambda e: e.tensor_add(
                    y2w[:, dc * TS:(dc + 1) * TS],
                    acc[:, dc * CC + ts * TS:dc * CC + (ts + 1) * TS],
                    l2f[tsb][:, dc * TS:(dc + 1) * TS]))(dc, ts, tsb, cc, y2w),
                     rd=(R_acc[dc], R_l2f[tsb]), wr=(R_y2w[dc],))

        def ln3_b1(cc, ts):
            bk7 = bank[6 + ts % 2]
            R_bk7 = R_bank[6 + ts % 2]
            tsb = ts % 2
            t0 = cc * CC + ts * TS
            y2w = y2wb[ts]
            R_y2w = R_y2wb[ts]
            mms = [(lambda dc, y2w: lambda e: e.matmul(
                bk7[0:1, 0:TS], ones_c.ap(), y2w[:, dc * TS:(dc + 1) * TS],
                start=(dc == 0), stop=(dc == 7)))(dc, y2w) for dc in range(8)]
            b.mmg(mms, R_bk7, [[R_const, R_y2w[dc]] for dc in range(8)])
            for dc in range(8):
                sl = dc % 2
                b.op("dve", (lambda dc, sl, y2w: lambda e: e.tensor_mul(
                    sq3[sl][:, :], y2w[:, dc * TS:(dc + 1) * TS],
                    y2w[:, dc * TS:(dc + 1) * TS]))(dc, sl, y2w),
                     rd=(R_y2w[dc],), wr=(R_sq3[sl],))
                fn = (lambda dc, sl: lambda e: e.matmul(
                    bk7[32:33, 0:TS], ones_c.ap(), sq3[sl][:, :],
                    start=(dc == 0), stop=(dc == 7)))(dc, sl)
                w = (R_bk7.wr() if dc == 0 else []) + R_const.rd() + R_sq3[sl].rd()
                tok = b.emit("pe", fn, w, "pe")
                R_const.addr(tok)
                R_sq3[sl].addr(tok)
                if dc == 7:
                    R_bk7.setw(tok)

        def ln3_b2(cc, ts):
            bk7 = bank[6 + ts % 2]
            R_bk7 = R_bank[6 + ts % 2]
            tsb = ts % 2
            y2w = y2wb[ts]
            R_y2w = R_y2wb[ts]
            rM = rows[0:1, 0 * T:0 * T + TS]
            rV = rows[0:1, 1 * T:1 * T + TS]
            rA = rows[0:1, 2 * T:2 * T + TS]
            rB = rows[0:1, 3 * T:3 * T + TS]
            rT_ = rows[0:1, 4 * T:4 * T + TS]
            rL = rT_
            b.op("dve", lambda e: e.tensor_scalar_mul(rM, bk7[0:1, 0:TS], 1.0 / D),
                 rd=(R_bk7,), wr=(R_rows,))
            b.op("dve", lambda e: e.tensor_scalar_mul(rV, bk7[32:33, 0:TS], 1.0 / D),
                 rd=(R_bk7,), wr=(R_rows,))
            b.op("dve", lambda e: e.tensor_mul(rT_, rM, rM), rd=(R_rows,), wr=(R_rows,))
            b.op("dve", lambda e: e.tensor_sub(rV, rV, rT_), rd=(R_rows,), wr=(R_rows,))
            b.op("act", lambda e: e.activation(rL, rV, AF.Ln, bias=epss[0:1, 0:1]),
                 rd=(R_rows,), wr=(R_rows,))
            b.op("act", lambda e: e.activation(rA, rL, AF.Exp, scale=-0.5),
                 rd=(R_rows,), wr=(R_rows,))
            b.op("dve", lambda e: e.scalar_tensor_tensor(rB, rM, -1.0, rA,
                                                         ALU.mult, ALU.mult),
                 rd=(R_rows,), wr=(R_rows,))
            b.mmg([lambda e: e.matmul(bk7[:, 0:TS], ones_r[0:1, 0:128], rA,
                                      start=True, stop=True)],
                  R_bk7, [[R_const, R_rows]])
            b.mmg([lambda e: e.matmul(bk7[:, 256:256 + TS], ones_r[0:1, 0:128], rB,
                                      start=True, stop=True)],
                  R_bk7, [[R_const, R_rows]])
            for dc in range(8):
                sl = dc % 2
                b.op("dve", (lambda dc, sl, y2w: lambda e: e.tensor_mul(
                    tb0[sl][:, :], y2w[:, dc * TS:(dc + 1) * TS],
                    bk7[:, 0:TS]))(dc, sl, y2w),
                     rd=(R_y2w[dc], R_bk7), wr=(R_tb0[sl],))
                b.op("dve", (lambda dc, sl: lambda e: e.tensor_add(
                    tb1[sl][:, :], tb0[sl][:, :], bk7[:, 256:256 + TS]))(dc, sl),
                     rd=(R_tb0[sl], R_bk7), wr=(R_tb1[sl],))
                b.op("act", (lambda dc, sl, y2w: lambda e: e.activation(
                    y2w[:, dc * TS:(dc + 1) * TS], tb1[sl][:, :], AF.Identity,
                    scale=gms[:, dc:dc + 1], bias=bts[:, dc:dc + 1]))(dc, sl, y2w),
                     rd=(R_tb1[sl], R_cst3), wr=(R_y2w[dc],))

        def ln3_b3(cc, ts):
            tsb = 0
            t0 = cc * CC + ts * TS
            y2w = y2wb[ts]
            R_y2w = R_y2wb[ts]
            b.op("sp", (lambda t0: lambda e: e.dma_start(
                y1f[0].ap().rearrange("p (dc t) -> p dc t", dc=8),
                y1_sp[:, t0:t0 + TS].rearrange("(dc p) t -> p dc t", p=128)))(t0),
                 rd=[r for qq in range(NT) for r in R_y1sp[qq]], wr=(R_y1f[0],))
            for dc in range(8):
                b.op("dve", (lambda dc, y2w: lambda e: e.tensor_add(
                    y2w[:, dc * TS:(dc + 1) * TS], y2w[:, dc * TS:(dc + 1) * TS],
                    y1f[0][:, dc * TS:(dc + 1) * TS]))(dc, y2w),
                     rd=(R_y1f[0],), wr=(R_y2w[dc],))
            for tt2 in range(2):
                ob = tt2 % 2
                for dc in range(8):
                    tb = dc % 4
                    b.mmg([(lambda dc, tt2, tb, y2w: lambda e: e.transpose(
                        bank[tb][:, 0:128],
                        y2w[:, dc * TS + tt2 * 128:dc * TS + (tt2 + 1) * 128],
                        id_f[:, :]))(dc, tt2, tb, y2w)],
                          R_bank[tb], [[R_y2w[dc], R_const]])
                    b.op("act", (lambda dc, ob, tb: lambda e: e.activation(
                        otk[ob][:, dc * 128:(dc + 1) * 128],
                        bank[tb][:, 0:128], AF.Identity))(dc, ob, tb),
                         rd=(R_bank[tb],), wr=(R_otk[ob],))
                trow = t0 + tt2 * 128
                b.op("sp", (lambda trow, ob: lambda e: e.dma_start(
                    out_d[trow:trow + 128, :], otk[ob][:, :]))(trow, ob),
                     rd=(R_otk[ob],))

        for ts in range(NTS):
            ln3_a(cc, ts)
        for k in range(NTS + 2):
            if k < NTS:
                ln3_b1(cc, k)
            if 1 <= k <= NTS:
                ln3_b2(cc, k - 1)
            if k >= 2:
                ln3_b3(cc, k - 2)

    b.emit("sp", None, [(s, b.counts[s]) for s in DMASEMS], None)

    with nc.Block() as block:
        def runner(eng_name):
            def run(engobj):
                incsz = {s: 16 for s in DMASEMS}
                for ws, fn, inc in b.stream[eng_name]:
                    for s, v in ws:
                        engobj.wait_ge(sems[s], v)
                    if fn is not None:
                        ins = fn(engobj)
                        if inc is not None:
                            ins.then_inc(sems[inc], incsz.get(inc, 1))
            return run
        block.tensor(runner("pe"))
        block.scalar(runner("act"))
        block.vector(runner("dve"))
        block.sync(runner("sp"))
        block.gpsimd(runner("pool"))
    return nc


def _consts(S):
    pos = np.arange(S, dtype=np.float64)
    j = np.arange(DK // 2, dtype=np.float64)
    rates = 10000.0 ** (-(2.0 * j) / DK)
    ang = pos[None, :] * rates[:, None]
    cos_t = np.cos(ang)
    sin_t = np.sin(ang)
    ctab = np.zeros((128, S), np.float32)
    stab = np.zeros((128, S), np.float32)
    for r in range(128):
        lj = (r % 64) // 2
        ctab[r] = cos_t[lj]
        stab[r] = (-sin_t[lj]) if (r % 64) % 2 == 0 else sin_t[lj]
    p2 = np.zeros((128, 128), np.float32)
    for m in range(128):
        p2[m ^ 1, m] = 1.0
    ident = np.eye(128, dtype=np.float32)
    masks = np.zeros((128, 4 * T), np.float32)
    pk = np.arange(128)[:, None]
    pq = np.arange(T)[None, :]
    for r in range(4):
        masks[:, r * T:(r + 1) * T] = (pk + 128 * r <= pq).astype(np.float32)
    return ctab, stab, p2, ident, masks


_NC_CACHE = {}


def kernel(L0_attn_x, L0_attn_Wq, L0_attn_Wk, L0_attn_Wv, L0_attn_Wo,
           L0_ffn_Wg, L0_ffn_bg, L0_ffn_W1, L0_ffn_b1, L0_ffn_W2, L0_ffn_b2,
           L0_ffn_gamma, L0_ffn_beta):
    x = np.asarray(L0_attn_x, np.float32)
    B, S, _ = x.shape
    assert B == 8 and _ == D
    ctab, stab, p2, ident, masks = _consts(S)
    common = dict(
        wqt=np.ascontiguousarray(np.asarray(L0_attn_Wq, np.float32).T),
        wkt=np.ascontiguousarray(np.asarray(L0_attn_Wk, np.float32).T),
        wvt=np.ascontiguousarray(np.asarray(L0_attn_Wv, np.float32).T),
        wot=np.ascontiguousarray(np.asarray(L0_attn_Wo, np.float32).T),
        wg=np.ascontiguousarray(np.asarray(L0_ffn_Wg, np.float32)),
        bgd=np.asarray(L0_ffn_bg, np.float32).reshape(1, NE),
        w1=np.ascontiguousarray(np.asarray(L0_ffn_W1, np.float32)),
        b1r=np.ascontiguousarray(
            np.asarray(L0_ffn_b1, np.float32).reshape(NE, 16, 128)
            .transpose(2, 0, 1).reshape(128, 128)),
        w2=np.ascontiguousarray(np.asarray(L0_ffn_W2, np.float32)),
        b2d=np.ascontiguousarray(np.asarray(L0_ffn_b2, np.float32)),
        gamd=np.ascontiguousarray(
            np.asarray(L0_ffn_gamma, np.float32).reshape(8, 128).T),
        betd=np.ascontiguousarray(
            np.asarray(L0_ffn_beta, np.float32).reshape(8, 128).T),
        ctabd=ctab, stabd=stab, p2d=p2, idd=ident, maskd=masks,
        onesd=np.ones((1, 128), np.float32),
        onescd=np.ones((128, 1), np.float32),
        epsd=np.full((1, 1), EPS, np.float32),
        onehd=np.ascontiguousarray(np.kron(np.eye(8, dtype=np.float32),
                                           np.ones((1, 128), np.float32))),
    )
    key = S
    if key not in _NC_CACHE:
        _NC_CACHE[key] = build(S)
    nc = _NC_CACHE[key]
    in_maps = []
    for c in range(8):
        m = dict(common)
        m["xT"] = np.ascontiguousarray(x[c].T)
        in_maps.append(m)
    res = run_bass_kernel_spmd(nc, in_maps, core_ids=list(range(8)))
    return np.stack([r["out"] for r in res.results], axis=0)


# version 26 (final)
# speedup vs baseline: 1.0081x; 1.0081x over previous
import numpy as np
import ml_dtypes
import concourse.bass as bass
import concourse.mybir as mybir
from concourse.bass_utils import run_bass_kernel_spmd

F32 = mybir.dt.float32
BF16 = mybir.dt.bfloat16
AF = mybir.ActivationFunctionType
ALU = mybir.AluOpType
AX = mybir.AxisListType.X
NPBF = ml_dtypes.bfloat16

DEBUG = False
D = 1024
H = 16
DK = 64
NE = 8
FF = 2048
T = 512
TS = 256
EPS = 1e-5

ENGS = ("pe", "act", "dve", "sp", "pool")
DMAQ = {"sp": ("dh0", "dh1"), "pool": ("ds0", "ds1")}
DMASEMS = ("dh0", "dh1", "ds0", "ds1")
CSEM = {"pe": "pe", "act": "act", "dve": "dve", "sp": "spc", "pool": "poolc"}
ALLSEMS = ("pe", "act", "dve", "spc", "poolc", "dh0", "dh1", "ds0", "ds1")


class Reg:
    __slots__ = ("w", "rs")

    def __init__(self):
        self.w = None
        self.rs = []

    def rd(self):
        return [self.w] if self.w is not None else []

    def wr(self):
        out = list(self.rs)
        if self.w is not None:
            out.append(self.w)
        return out

    def setw(self, tok):
        if tok is not None:
            self.w = tok
            self.rs = []

    def addr(self, tok):
        if tok is not None:
            self.rs.append(tok)


class Bld:
    def __init__(self):
        self.counts = {s: 0 for s in ALLSEMS}
        self.stream = {e: [] for e in ENGS}
        self.wm = {e: {s: 0 for s in ALLSEMS} for e in ENGS}
        self.dmaq_idx = {"sp": 0, "pool": 0}

    def emit(self, eng, fn, waits, inc):
        ws = {}
        for t in waits:
            if t is None:
                continue
            s, v = t
            if v > self.wm[eng][s] and v > ws.get(s, 0):
                ws[s] = v
        for s, v in ws.items():
            self.wm[eng][s] = v
        tok = None
        if inc is not None:
            n = 16 if inc in DMASEMS else 1
            self.counts[inc] += n
            tok = (inc, self.counts[inc])
        self.stream[eng].append((tuple(sorted(ws.items())), fn, inc))
        return tok

    def op(self, eng, fn, rd=(), wr=(), sem=0, extra=()):
        waits = list(extra)
        if sem == 0:
            if eng in DMAQ:
                i = self.dmaq_idx[eng]
                self.dmaq_idx[eng] = 1 - i
                sem = DMAQ[eng][i]
                waits.append((sem, self.counts[sem]))
            else:
                sem = CSEM[eng]
        for r in rd:
            waits += r.rd()
        for r in wr:
            waits += r.wr()
        tok = self.emit(eng, fn, waits, sem)
        for r in rd:
            r.addr(tok)
        for r in wr:
            r.setw(tok)
        return tok

    def mmg(self, mms, wr_reg, rd_each):
        n = len(mms)
        all_rd = []
        tok = None
        for i, (fn, rds) in enumerate(zip(mms, rd_each)):
            waits = list(wr_reg.wr()) if i == 0 else []
            for r in rds:
                waits += r.rd()
                all_rd.append(r)
            tok = self.emit("pe", fn, waits, "pe" if i == n - 1 else None)
        for r in all_rd:
            r.addr(tok)
        wr_reg.setw(tok)
        return tok

    def barrier(self):
        for e in ENGS:
            self.emit(e, lambda eng: eng.nop(),
                      [(CSEM[e], self.counts[CSEM[e]])], CSEM[e])
        snap = dict(self.counts)
        waits = [(s, snap[s]) for s in ALLSEMS]
        for e in ENGS:
            self.emit(e, None, waits, None)


def build(S):
    assert S % T == 0
    NT = S // T
    CC = 1024 if S % 1024 == 0 else S
    NCC = S // CC
    NTS = CC // TS
    NKT = S // 128

    nc = bass.Bass("TRN2", target_bir_lowering=False)

    xT = nc.dram_tensor("xT", [D, S], F32, kind="ExternalInput").ap()
    wqt = nc.dram_tensor("wqt", [D, D], F32, kind="ExternalInput").ap()
    wkt = nc.dram_tensor("wkt", [D, D], F32, kind="ExternalInput").ap()
    wvt = nc.dram_tensor("wvt", [D, D], F32, kind="ExternalInput").ap()
    wot = nc.dram_tensor("wot", [D, D], F32, kind="ExternalInput").ap()
    wg = nc.dram_tensor("wg", [D, NE], F32, kind="ExternalInput").ap()
    bgd = nc.dram_tensor("bgd", [1, NE], F32, kind="ExternalInput").ap()
    w1 = nc.dram_tensor("w1", [NE, D, FF], F32, kind="ExternalInput").ap()
    b1r = nc.dram_tensor("b1r", [128, 128], F32, kind="ExternalInput").ap()
    w2 = nc.dram_tensor("w2", [NE, FF, D], F32, kind="ExternalInput").ap()
    b2d = nc.dram_tensor("b2d", [NE, D], F32, kind="ExternalInput").ap()
    gamd = nc.dram_tensor("gamd", [128, 8], F32, kind="ExternalInput").ap()
    betd = nc.dram_tensor("betd", [128, 8], F32, kind="ExternalInput").ap()
    ctabd = nc.dram_tensor("ctabd", [128, S], F32, kind="ExternalInput").ap()
    stabd = nc.dram_tensor("stabd", [128, S], F32, kind="ExternalInput").ap()
    p2d = nc.dram_tensor("p2d", [128, 128], F32, kind="ExternalInput").ap()
    idd = nc.dram_tensor("idd", [128, 128], F32, kind="ExternalInput").ap()
    maskd = nc.dram_tensor("maskd", [128, 4 * T], F32, kind="ExternalInput").ap()
    onesd = nc.dram_tensor("onesd", [1, 128], F32, kind="ExternalInput").ap()
    onescd = nc.dram_tensor("onescd", [128, 1], F32, kind="ExternalInput").ap()
    epsd = nc.dram_tensor("epsd", [1, 1], F32, kind="ExternalInput").ap()
    onehd = nc.dram_tensor("onehd", [8, 8 * 128], F32, kind="ExternalInput").ap()
    out_d = nc.dram_tensor("out", [S, D], F32, kind="ExternalOutput").ap()

    if DEBUG:
        dbg_ln1 = nc.dram_tensor("dbg_ln1", [D, S], BF16, kind="Internal").ap()
        dbg_rows = nc.dram_tensor("dbg_rows", [S // T, 6 * T], F32, kind="Internal").ap()
    q_sp = nc.dram_tensor("q_sp", [D, S], BF16, kind="Internal").ap()
    k_sp = nc.dram_tensor("k_sp", [D, S], BF16, kind="Internal").ap()
    v_sp = nc.dram_tensor("v_sp", [NKT, 128, H * 65], BF16, kind="Internal").ap()
    ln2_sp = nc.dram_tensor("ln2_sp", [D, S], F32, kind="Internal").ap()
    y1_sp = nc.dram_tensor("y1_sp", [D, S], F32, kind="Internal").ap()
    wt_sp = nc.dram_tensor("wt_sp", [NE, S], F32, kind="Internal").ap()

    b = Bld()

    ones_r = nc.alloc_sbuf_tensor("ones_r", [1, 128], F32)
    ones_b = nc.alloc_sbuf_tensor("ones_b", [1, 128], BF16)
    ones_c = nc.alloc_sbuf_tensor("ones_c", [128, 1], F32)
    id_f = nc.alloc_sbuf_tensor("id_f", [128, 128], F32)
    rows = nc.alloc_sbuf_tensor("rows", [1, 5 * T], F32)
    epss = nc.alloc_sbuf_tensor("epss", [1, 1], F32)
    bank = [nc.alloc_psum_tensor(f"bank{i}", [128, 512], F32) for i in range(8)]
    sems = {s: nc.alloc_semaphore(s) for s in ALLSEMS}

    _fence = nc.alloc_sbuf_tensor("fence", [1, 8], F32)
    FREE0 = (nc.lookup_mloc(_fence).addr + 63) & ~31
    cur = [FREE0]

    def salloc(name, shape, dt):
        width = int(np.prod(shape[1:]))
        nbytes = (width * (2 if dt == BF16 else 4) + 31) & ~31
        off = cur[0]
        cur[0] += nbytes
        assert cur[0] <= 223 * 1024, f"SBUF overflow at {name}: {cur[0] // 1024} KiB"
        return nc.alloc_sbuf_tensor_at(name, list(shape), dt, offset=off)

    R_const = Reg()
    R_rows = Reg()
    R_bank = [Reg() for _ in range(8)]

    def emit_ln(get_x, x_regs, width, sqB, R_sqB, bs=2, ba=3):
        bk2, bk3 = bank[bs], bank[ba]
        x_ap = [get_x(dc) for dc in range(8)]
        mms = [(lambda xa, dc: lambda e: e.matmul(
            bk2[0:1, 0:width], ones_c.ap(), xa,
            start=(dc == 0), stop=(dc == 7)))(x_ap[dc], dc) for dc in range(8)]
        b.mmg(mms, R_bank[bs], [[R_const, x_regs[dc]] for dc in range(8)])
        for dc in range(8):
            sl = dc % 2
            b.op("dve", (lambda xa, sl: lambda e: e.tensor_mul(
                sqB[sl][:, 0:width], xa, xa))(x_ap[dc], sl),
                 rd=(x_regs[dc],), wr=(R_sqB[sl],))
            fn = (lambda dc, sl: lambda e: e.matmul(
                bk2[32:33, 0:width], ones_c.ap(), sqB[sl][:, 0:width],
                start=(dc == 0), stop=(dc == 7)))(dc, sl)
            w = (R_bank[bs].wr() if dc == 0 else []) + R_const.rd() + R_sqB[sl].rd()
            tok = b.emit("pe", fn, w, "pe")
            R_const.addr(tok)
            R_sqB[sl].addr(tok)
            if dc == 7:
                R_bank[bs].setw(tok)
        rM = rows[0:1, 0 * T:0 * T + width]
        rV = rows[0:1, 1 * T:1 * T + width]
        rA = rows[0:1, 2 * T:2 * T + width]
        rB = rows[0:1, 3 * T:3 * T + width]
        rT_ = rows[0:1, 4 * T:4 * T + width]
        rL = rT_
        b.op("dve", lambda e: e.tensor_scalar_mul(rM, bank[bs][0:1, 0:width], 1.0 / D),
             rd=(R_bank[bs],), wr=(R_rows,))
        b.op("dve", lambda e: e.tensor_scalar_mul(rV, bank[bs][32:33, 0:width], 1.0 / D),
             rd=(R_bank[bs],), wr=(R_rows,))
        b.op("dve", lambda e: e.tensor_mul(rT_, rM, rM), rd=(R_rows,), wr=(R_rows,))
        b.op("dve", lambda e: e.tensor_sub(rV, rV, rT_), rd=(R_rows,), wr=(R_rows,))
        b.op("act", lambda e: e.activation(rL, rV, AF.Ln, bias=epss[0:1, 0:1]),
             rd=(R_rows,), wr=(R_rows,))
        b.op("act", lambda e: e.activation(rA, rL, AF.Exp, scale=-0.5),
             rd=(R_rows,), wr=(R_rows,))
        b.op("dve", lambda e: e.scalar_tensor_tensor(rB, rM, -1.0, rA,
                                                     ALU.mult, ALU.mult),
             rd=(R_rows,), wr=(R_rows,))
        b.mmg([lambda e: e.matmul(bk3[:, 0:width], ones_r[0:1, 0:128], rA,
                                  start=True, stop=True)],
              R_bank[ba], [[R_const, R_rows]])
        b.mmg([lambda e: e.matmul(bk2[:, 0:width], ones_r[0:1, 0:128], rB,
                                  start=True, stop=True)],
              R_bank[bs], [[R_const, R_rows]])

    cs = salloc("cs_tab", [128, S], F32)
    ss = salloc("ss_tab", [128, S], F32)
    p2s = salloc("p2s", [128, 128], F32)
    wq_s = salloc("wq_s", [128, 8 * 1024], BF16)
    wk_s = salloc("wk_s", [128, 8 * 1024], BF16)
    wv_s = salloc("wv_s", [128, 8 * 1024], BF16)
    xin = [salloc(f"xin{i}", [128, 8 * T], F32) for i in range(2)]
    ln1 = salloc("ln1", [128, 8 * T], BF16)
    sq1 = [salloc(f"sq1_{i}", [128, T], F32) for i in range(2)]
    qraw = [salloc(f"qraw{i}", [128, T], F32) for i in range(2)]
    tm0 = [salloc(f"tm0{i}", [128, T], F32) for i in range(2)]
    tm1 = [salloc(f"tm1{i}", [128, T], F32) for i in range(2)]
    qst = salloc("qst", [128, 8 * T], BF16)
    kst = salloc("kst", [128, 8 * T], BF16)
    vfm = [salloc(f"vfm{i}", [128, T], F32) for i in range(2)]
    vst = salloc("vst", [128, 4 * 1040], BF16)

    R_xin = [Reg(), Reg()]
    R_ln1 = [Reg() for _ in range(8)]
    R_sq1 = [Reg(), Reg()]
    R_qraw = [Reg(), Reg()]
    R_tm0 = [Reg(), Reg()]
    R_tm1 = [Reg(), Reg()]
    R_qst = [Reg() for _ in range(8)]
    R_kst = [Reg() for _ in range(8)]
    R_vfm = [Reg(), Reg()]
    R_vst = [Reg() for _ in range(4)]
    R_qsp = [Reg() for _ in range(NT)]
    R_ksp = [Reg() for _ in range(NT)]
    R_vsp = [Reg() for _ in range(NT)]

    b.op("dve", lambda e: e.memset(vst[:, :], 1.0), wr=tuple(R_vst))

    b.op("pool", lambda e: e.dma_start(ones_b[:, :], onesd), wr=(R_const,))
    R_warm = Reg()
    b.op("dve", lambda e: e.memset(kst[0:1, 0:512], 0.0), wr=(R_warm,))
    for wi_ in range(64):
        b.op("pe", (lambda: lambda e: e.matmul(
            bank[7][:, :], ones_b[0:1, 0:128], kst[0:1, 0:512],
            start=True, stop=True))(),
             rd=(R_const, R_warm))
    for dst, src in ((ones_r, onesd), (ones_c, onescd), (id_f, idd),
                     (cs, ctabd), (ss, stabd), (p2s, p2d), (epss, epsd)):
        b.op("sp", (lambda d, s: lambda e: e.dma_start(d[:, :], s))(dst, src),
             wr=(R_const,))
    for dst, src in ((wq_s, wqt), (wk_s, wkt), (wv_s, wvt)):
        b.op("pool", (lambda d, s: lambda e: e.dma_start(
            d.ap().rearrange("p (kc d) -> p kc d", kc=8),
            s.rearrange("(kc p) d -> p kc d", p=128)))(dst, src),
             wr=(R_const,))

    for tc in range(NT):
        xb = tc % 2
        b.op("sp", (lambda tc, xb: lambda e: e.dma_start(
            xin[xb].ap().rearrange("p (dc t) -> p dc t", dc=8),
            xT[:, tc * T:(tc + 1) * T].rearrange("(dc p) t -> p dc t", p=128)))(tc, xb),
             wr=(R_xin[xb],))
        emit_ln(lambda dc: xin[xb][:, dc * T:(dc + 1) * T], [R_xin[xb]] * 8,
                T, sq1, R_sq1)
        if DEBUG:
            b.op("sp", (lambda tc: lambda e: e.dma_start(
                dbg_rows[tc:tc + 1, :], rows[0:1, :]))(tc), rd=(R_rows,))
        for dc in range(8):
            sl = dc % 2
            b.op("dve", (lambda dc, sl, xb: lambda e: e.tensor_mul(
                tm0[sl][:, :], xin[xb][:, dc * T:(dc + 1) * T], bank[3][:, :]))(dc, sl, xb),
                 rd=(R_xin[xb], R_bank[3]), wr=(R_tm0[sl],))
            b.op("dve", (lambda dc, sl: lambda e: e.tensor_add(
                ln1[:, dc * T:(dc + 1) * T], tm0[sl][:, :], bank[2][:, :]))(dc, sl),
                 rd=(R_tm0[sl], R_bank[2]), wr=(R_ln1[dc],))

        def proj_group(wi, dc):
            wsb = (wq_s, wk_s, wv_s)[wi]
            g = (wi * 8 + dc) % 2
            mms = [(lambda kc, dc, wsb, g: lambda e: e.matmul(
                bank[g][:, :],
                wsb[:, kc * 1024 + dc * 128:kc * 1024 + (dc + 1) * 128],
                ln1[:, kc * T:(kc + 1) * T],
                start=(kc == 0), stop=(kc == 7)))(kc, dc, wsb, g)
                for kc in range(8)]
            b.mmg(mms, R_bank[g], [[R_const, R_ln1[kc]] for kc in range(8)])

        def proj_post(wi, dc, tc):
            g = (wi * 8 + dc) % 2
            if wi < 2:
                qs = dc % 2
                b.op("act", (lambda qs, g: lambda e: e.activation(
                    qraw[qs][:, :], bank[g][:, :], AF.Identity))(qs, g),
                     rd=(R_bank[g],), wr=(R_qraw[qs],))
                sb_i = 5 + (dc % 2)
                b.mmg([(lambda qs, sb_i: lambda e: e.matmul(
                    bank[sb_i][:, :], p2s[:, :], qraw[qs][:, :],
                    start=True, stop=True))(qs, sb_i)],
                      R_bank[sb_i], [[R_const, R_qraw[qs]]])
                b.op("dve", (lambda qs, tc: lambda e: e.tensor_mul(
                    tm0[qs][:, :], qraw[qs][:, :], cs[:, tc * T:(tc + 1) * T]))(qs, tc),
                     rd=(R_qraw[qs], R_const), wr=(R_tm0[qs],))
                b.op("dve", (lambda qs, sb_i, tc: lambda e: e.tensor_mul(
                    tm1[qs][:, :], bank[sb_i][:, :], ss[:, tc * T:(tc + 1) * T]))(qs, sb_i, tc),
                     rd=(R_bank[sb_i], R_const), wr=(R_tm1[qs],))
                stage = qst if wi == 0 else kst
                sreg = (R_qst if wi == 0 else R_kst)[dc]
                b.op("dve", (lambda qs, dc, stage: lambda e: e.tensor_add(
                    stage[:, dc * T:(dc + 1) * T], tm0[qs][:, :], tm1[qs][:, :]))(qs, dc, stage),
                     rd=(R_tm0[qs], R_tm1[qs]), wr=(sreg,))
            else:
                vs = dc % 2
                b.op("act", (lambda vs, g: lambda e: e.activation(
                    vfm[vs][:, :], bank[g][:, :], AF.Identity))(vs, g),
                     rd=(R_bank[g],), wr=(R_vfm[vs],))
                for tt in range(4):
                    tb = 4 if (dc * 4 + tt) % 2 == 0 else 7
                    b.mmg([(lambda vs, tt, tb: lambda e: e.transpose(
                        bank[tb][:, 0:128],
                        vfm[vs][:, tt * 128:(tt + 1) * 128], id_f[:, :]))(vs, tt, tb)],
                          R_bank[tb], [[R_vfm[vs], R_const]])
                    for hs2 in range(2):
                        hcol = 65 * (2 * dc + hs2)
                        b.op("dve", (lambda tt, tb, hs2, hcol: lambda e: e.tensor_copy(
                            vst[:, tt * 1040 + hcol:tt * 1040 + hcol + 64],
                            bank[tb][:, hs2 * 64:hs2 * 64 + 64]))(tt, tb, hs2, hcol),
                             rd=(R_bank[tb],), wr=(R_vst[tt],))

        items = [(wi, dc) for wi in range(3) for dc in range(8)]
        proj_group(*items[0])
        for i in range(24):
            if i + 1 < 24:
                proj_group(*items[i + 1])
            proj_post(*items[i], tc)
        b.op("sp", (lambda tc: lambda e: e.dma_start(
            q_sp[:, tc * T:(tc + 1) * T].rearrange("(dc p) t -> p dc t", p=128),
            qst.ap().rearrange("p (dc t) -> p dc t", dc=8)))(tc),
             rd=tuple(R_qst), wr=(R_qsp[tc],))
        b.op("sp", (lambda tc: lambda e: e.dma_start(
            k_sp[:, tc * T:(tc + 1) * T].rearrange("(dc p) t -> p dc t", p=128),
            kst.ap().rearrange("p (dc t) -> p dc t", dc=8)))(tc),
             rd=tuple(R_kst), wr=(R_ksp[tc],))
        b.op("sp", (lambda tc: lambda e: e.dma_start(
            v_sp[tc * 4:(tc + 1) * 4].rearrange("tt p c -> p tt c"),
            vst.ap().rearrange("p (tt c) -> p tt c", tt=4)))(tc),
             rd=tuple(R_vst), wr=(R_vsp[tc],))

    b.barrier()
    cur[0] = FREE0
    kt_s = salloc("kt_s", [128, 8 * S], BF16)
    vh_s = salloc("vh_s", [128, NKT * 1040], BF16)
    qt_s = salloc("qt_s", [128, 8 * T], BF16)
    wo_dc = [salloc(f"wo_dc{i}", [128, 8 * 128], BF16) for i in range(2)]
    mk_s = salloc("mk_s", [128, 4 * T], BF16)
    pr_s = [salloc(f"pr{i}", [128, T], BF16) for i in range(3)]
    _xy1 = salloc("xy1", [128, 8 * T], F32)
    xy1b = [_xy1, _xy1]
    att_b = [salloc(f"att_s{i}", [128, 8 * T], BF16) for i in range(2)]
    rb_s = salloc("rb_s", [128, T], F32)
    rr_s = salloc("rr_s", [1, 2 * T], F32)
    rr_b = salloc("rr_b", [1, 2 * T], BF16)
    _sq2 = salloc("sq2", [128, T], F32)
    sq2 = [_sq2, _sq2]
    wgs = salloc("wgs", [128, 8 * 8], F32)
    bgs = salloc("bgs", [1, 8], F32)
    rt = salloc("rt", [128, 6 * 8], F32)
    rt1 = salloc("rt1", [128, 8], F32)
    wst = salloc("wst", [8, T], F32)

    R_kts = [Reg() for _ in range(8)]
    R_vhs = [Reg() for _ in range(NKT)]
    R_vones = Reg()
    R_qts = Reg()
    R_wodc = [Reg(), Reg()]
    R_mks = Reg()
    R_prs = [Reg() for _ in range(3)]
    _R_xy1 = [Reg() for _ in range(8)]
    R_xy1b = [_R_xy1, _R_xy1]
    R_attb = [[Reg() for _ in range(16)] for _ in range(2)]
    R_rbs = Reg()
    R_rr1 = Reg()
    R_rrs = [R_rr1, R_rr1]
    R_rrx = Reg()
    R_attf = [Reg(), Reg()]
    _R_sq2 = Reg()
    R_sq2 = [_R_sq2, _R_sq2]
    R_wgs = Reg()
    R_rt = [Reg() for _ in range(6)]
    R_rt1 = Reg()
    R_wst = [Reg() for _ in range(4)]
    R_l2sp = [[Reg() for _ in range(8)] for _ in range(NT)]
    R_y1sp = [[Reg() for _ in range(8)] for _ in range(NT)]
    R_wtsp = [Reg() for _ in range(NT)]

    for hp in range(8):
        b.op("sp", (lambda hp: lambda e: e.dma_start(
            kt_s[:, hp * S:(hp + 1) * S], k_sp[hp * 128:(hp + 1) * 128, :]))(hp),
             rd=tuple(R_ksp), wr=(R_kts[hp],))
    ng = (NKT + 7) // 8
    for g in range(ng):
        n = min(8, NKT - g * 8)
        b.op("pool", (lambda g, n: lambda e: e.dma_start(
            vh_s[:, g * 8 * 1040:(g * 8 + n) * 1040].rearrange(
                "p (kt c) -> p kt c", kt=n),
            v_sp[g * 8:g * 8 + n].rearrange("kt p c -> p kt c")))(g, n),
             rd=tuple(R_vsp), wr=tuple(R_vhs[g * 8:g * 8 + n]))
    b.op("pool", lambda e: e.dma_start(mk_s[:, :], maskd), wr=(R_mks,))
    b.op("sp", lambda e: e.dma_start(
        wgs.ap().rearrange("p (dc e8) -> p dc e8", dc=8),
        wg.rearrange("(dc p) e8 -> p dc e8", p=128)), wr=(R_wgs,))
    b.op("sp", lambda e: e.dma_start(bgs[:, :], bgd), wr=(R_wgs,))

    def make_tail(qc, par):
        xy1 = xy1b[par]
        att_s = att_b[par]
        R_xy1 = R_xy1b[par]
        R_att = R_attb[par]
        t = []

        def wo_load(dc):
            b.op("pool", (lambda dc: lambda e: e.dma_start(
                wo_dc[dc % 2].ap().rearrange("p (kc d) -> p kc d", kc=8),
                wot[:, dc * 128:(dc + 1) * 128].rearrange(
                    "(kc p) d -> p kc d", p=128)))(dc),
                 wr=(R_wodc[dc % 2],))

        def x_load():
            b.op("sp", (lambda qc: lambda e: e.dma_start(
                xy1.ap().rearrange("p (dc t) -> p dc t", dc=8),
                xT[:, qc * T:(qc + 1) * T].rearrange("(dc p) t -> p dc t", p=128)))(qc),
                 wr=tuple(R_xy1))
        t.append(lambda: (x_load(), wo_load(0), wo_load(1)))

        def wo_dc_step(dc):
            g = 6 + dc % 2
            mms = [(lambda kc, dc, g: lambda e: e.matmul(
                bank[g][:, :], wo_dc[dc % 2][:, kc * 128:(kc + 1) * 128],
                att_s[:, kc * T:(kc + 1) * T],
                start=(kc == 0), stop=(kc == 7)))(kc, dc, g) for kc in range(8)]
            b.mmg(mms, R_bank[g],
                  [[R_wodc[dc % 2], R_att[2 * kc], R_att[2 * kc + 1]] for kc in range(8)])
            if dc + 2 < 8:
                wo_load(dc + 2)
            b.op("dve", (lambda dc, g: lambda e: e.tensor_add(
                xy1[:, dc * T:(dc + 1) * T], bank[g][:, :],
                xy1[:, dc * T:(dc + 1) * T]))(dc, g),
                 rd=(R_bank[g],), wr=(R_xy1[dc],))
            b.op("sp", (lambda dc, qc: lambda e: e.dma_start(
                y1_sp[dc * 128:(dc + 1) * 128, qc * T:(qc + 1) * T],
                xy1[:, dc * T:(dc + 1) * T]))(dc, qc),
                 rd=(R_xy1[dc],), wr=(R_y1sp[qc][dc],))

        for dc in range(8):
            t.append((lambda dc: lambda: wo_dc_step(dc))(dc))

        def ln2_all():
            emit_ln(lambda dc: xy1[:, dc * T:(dc + 1) * T],
                    R_xy1, T, sq2, R_sq2, bs=6, ba=7)
            for dc in range(8):
                ln2_apply(dc)

        def ln2_apply(dc):
            b.op("dve", (lambda dc: lambda e: e.tensor_mul(
                xy1[:, dc * T:(dc + 1) * T], xy1[:, dc * T:(dc + 1) * T],
                bank[7][:, :]))(dc),
                 rd=(R_bank[7],), wr=(R_xy1[dc],),
                 extra=R_y1sp[qc][dc].rd())
            b.op("dve", (lambda dc: lambda e: e.tensor_add(
                xy1[:, dc * T:(dc + 1) * T], xy1[:, dc * T:(dc + 1) * T],
                bank[6][:, :]))(dc),
                 rd=(R_bank[6],), wr=(R_xy1[dc],))
            b.op("sp", (lambda dc, qc: lambda e: e.dma_start(
                ln2_sp[dc * 128:(dc + 1) * 128, qc * T:(qc + 1) * T],
                xy1[:, dc * T:(dc + 1) * T]))(dc, qc),
                 rd=(R_xy1[dc],), wr=(R_l2sp[qc][dc],))

        t.append(ln2_all)

        def router_tt(tt):
            mms = [(lambda dc, tt: lambda e: e.matmul(
                bank[6][0:128, 0:8],
                xy1[:, dc * T + tt * 128:dc * T + (tt + 1) * 128],
                wgs[:, dc * 8:(dc + 1) * 8],
                start=(dc == 0), stop=False))(dc, tt) for dc in range(8)]
            mms.append(lambda e: e.matmul(
                bank[6][0:128, 0:8], ones_r[0:1, 0:128], bgs[:, :],
                start=False, stop=True))
            b.mmg(mms, R_bank[6],
                  [[R_xy1[dc], R_wgs] for dc in range(8)] + [[R_const, R_wgs]])
            c = lambda i: rt[:, i * 8:(i + 1) * 8]
            v1c, v2c = rt1[:, 0:1], rt1[:, 1:2]
            w2c, w1c, tc1 = rt1[:, 2:3], rt1[:, 3:4], rt1[:, 4:5]
            b.op("dve", lambda e: e.reduce_max(v1c, bank[6][0:128, 0:8], axis=AX),
                 rd=(R_bank[6],), wr=(R_rt1,))
            b.op("dve", lambda e: e.tensor_scalar(c(0), bank[6][0:128, 0:8],
                                                  v1c, None, ALU.subtract),
                 rd=(R_bank[6], R_rt1), wr=(R_rt[0],))
            b.op("dve", lambda e: e.tensor_scalar(c(1), c(0), 0.0, None, ALU.is_equal),
                 rd=(R_rt[0],), wr=(R_rt[1],))
            b.op("dve", lambda e: e.scalar_tensor_tensor(
                c(2), c(1), -1e9, c(0), ALU.mult, ALU.add),
                 rd=(R_rt[0], R_rt[1]), wr=(R_rt[2],))
            b.op("dve", lambda e: e.reduce_max(v2c, c(2), axis=AX),
                 rd=(R_rt[2],), wr=(R_rt1,))
            b.op("act", lambda e: e.activation(tc1, v2c, AF.Exp, scale=-1.0),
                 rd=(R_rt1,), wr=(R_rt1,))
            b.op("dve", lambda e: e.tensor_scalar_add(tc1, tc1, 1.0),
                 rd=(R_rt1,), wr=(R_rt1,))
            b.op("dve", lambda e: e.reciprocal(w2c, tc1), rd=(R_rt1,), wr=(R_rt1,))
            b.op("dve", lambda e: e.tensor_scalar(w1c, w2c, -1.0, 1.0,
                                                  ALU.mult, ALU.add),
                 rd=(R_rt1,), wr=(R_rt1,))
            b.op("dve", lambda e: e.tensor_scalar(c(3), c(0), v2c, None, ALU.is_equal),
                 rd=(R_rt[0], R_rt1), wr=(R_rt[3],))
            b.op("dve", lambda e: e.tensor_scalar(c(4), c(3), w2c, None, ALU.mult),
                 rd=(R_rt[3], R_rt1), wr=(R_rt[4],))
            b.op("dve", lambda e: e.scalar_tensor_tensor(
                c(5), c(1), w1c, c(4), ALU.mult, ALU.add),
                 rd=(R_rt[1], R_rt1, R_rt[4]), wr=(R_rt[5],))
        def router_tp(tt):
            c = lambda i: rt[:, i * 8:(i + 1) * 8]
            b.mmg([(lambda: lambda e: e.transpose(
                bank[7][0:8, 0:128], c(5), id_f[:, :]))()],
                  R_bank[7], [[R_rt[5], R_const]])
            b.op("dve", (lambda tt: lambda e: e.tensor_copy(
                wst[:, tt * 128:(tt + 1) * 128], bank[7][0:8, 0:128]))(tt),
                 rd=(R_bank[7],), wr=(R_wst[tt],))

        for tt in range(4):
            t.append((lambda tt: lambda: router_tt(tt))(tt))
            t.append((lambda tt: lambda: router_tp(tt))(tt))
        t.append(lambda: b.op("sp", (lambda qc: lambda e: e.dma_start(
            wt_sp[:, qc * T:(qc + 1) * T], wst[:, :]))(qc),
            rd=tuple(R_wst), wr=(R_wtsp[qc],)))
        return t

    pend = []

    def drip(k):
        for _ in range(min(k, len(pend))):
            pend.pop(0)()

    for qc in range(NT):
        nk = 4 * (qc + 1)
        par = qc % 2
        att_s = att_b[par]
        R_att = R_attb[par]
        b.op("sp", (lambda qc: lambda e: e.dma_start(
            qt_s.ap().rearrange("p (hp t) -> p hp t", hp=8),
            q_sp[:, qc * T:(qc + 1) * T].rearrange("(hp p) t -> p hp t", p=128)))(qc),
             rd=(R_qsp[qc],), wr=(R_qts,))
        pending_norm = []
        for hp in range(8):
            def s_mm(kt, hs):
                bi = (2 * kt + hs) % 4
                fn = (lambda kt, hs, bi, hp: lambda e: e.matmul(
                    bank[bi][:, :],
                    kt_s[hs * 64:hs * 64 + 64,
                         hp * S + kt * 128:hp * S + (kt + 1) * 128],
                    qt_s[hs * 64:hs * 64 + 64, hp * T:(hp + 1) * T],
                    start=True, stop=True,
                    tile_position=(hs * 64, 0)))(kt, hs, bi, hp)
                b.mmg([fn], R_bank[bi], [[R_kts[hp], R_qts]])

            s_mm(0, 0)
            s_mm(0, 1)
            for kt in range(nk):
                if kt + 1 < nk:
                    s_mm(kt + 1, 0)
                    s_mm(kt + 1, 1)
                if kt == 1 or nk == 1:
                    for nf in pending_norm:
                        nf()
                    pending_norm = []
                if (kt == 2 and hp >= 1) or (kt == 0 and nk <= 2 and hp >= 1):
                    drip(5)
                for hs in range(2):
                    h = 2 * hp + hs
                    bi = (2 * kt + hs) % 4
                    pa = bank[4 + hs]
                    rpa = R_bank[4 + hs]
                    slot = (kt * 2 + hs) % 3
                    b.op("act", (lambda bi, slot: lambda e: e.activation(
                        pr_s[slot][:, :], bank[bi][:, :], AF.Exp, scale=0.125))(bi, slot),
                         rd=(R_bank[bi],), wr=(R_prs[slot],))
                    if kt >= 4 * qc:
                        r = kt - 4 * qc
                        b.op("dve", (lambda r, slot: lambda e: e.tensor_mul(
                            pr_s[slot][:, :], pr_s[slot][:, :],
                            mk_s[:, r * T:(r + 1) * T]))(r, slot),
                             rd=(R_mks,), wr=(R_prs[slot],))
                    fn = (lambda kt, slot, h, pa, nk: lambda e: e.matmul(
                        pa[0:65, :],
                        vh_s[:, kt * 1040 + 65 * h:kt * 1040 + 65 * h + 65],
                        pr_s[slot][:, :],
                        start=(kt == 0), stop=(kt == nk - 1)))(kt, slot, h, pa, nk)
                    w = list(rpa.wr()) if kt == 0 else []
                    w += R_vhs[kt].rd() + R_vones.rd() + R_prs[slot].rd()
                    tok = b.emit("pe", fn, w, "pe")
                    R_vhs[kt].addr(tok)
                    R_vones.addr(tok)
                    R_prs[slot].addr(tok)
                    if kt == nk - 1:
                        rpa.setw(tok)
            for hs in range(2):
                pa = bank[4 + hs]
                rpa = R_bank[4 + hs]
                rr_l = rr_s[0:1, hs * T:(hs + 1) * T]
                aslice = att_s[hs * 64:hs * 64 + 64, hp * T:(hp + 1) * T]
                areg = R_att[2 * hp + hs]
                rr = rr_b[0:1, hs * T:(hs + 1) * T]
                b.op("act", (lambda rr_l, pa: lambda e: e.activation(
                    rr_l, pa[64:65, :], AF.Ln))(rr_l, pa),
                     rd=(rpa,), wr=(R_rrs[hs], R_attf[hs]))
                b.op("dve", (lambda aslice, pa: lambda e: e.tensor_copy(
                    aslice, pa[0:64, :]))(aslice, pa),
                     rd=(rpa, R_attf[hs]), wr=(areg,))
                b.op("act", (lambda rr, rr_l: lambda e: e.activation(
                    rr, rr_l, AF.Exp, scale=-1.0))(rr, rr_l),
                     rd=(R_rrs[hs],), wr=(R_rrx,))

                def norm_fin(hs, rr, aslice, areg):
                    rbb = 6 + hs
                    p0 = hs * 64
                    b.mmg([(lambda rr, rbb, p0: lambda e: e.matmul(
                        bank[rbb][p0:p0 + 64, :], ones_b[0:1, 0:64], rr,
                        start=True, stop=True,
                        tile_position=(0, p0)))(rr, rbb, p0)],
                          R_bank[rbb], [[R_const, R_rrx]])
                    b.op("dve", (lambda rbb, p0: lambda e: e.tensor_copy(
                        rb_s[p0:p0 + 64, :], bank[rbb][p0:p0 + 64, :]))(rbb, p0),
                         rd=(R_bank[rbb],), wr=(R_rbs,))
                    b.op("dve", (lambda aslice, p0: lambda e: e.tensor_mul(
                        aslice, aslice, rb_s[p0:p0 + 64, :]))(aslice, p0),
                         rd=(R_rbs,), wr=(areg,))

                pending_norm.append(
                    (lambda hs, rr, aslice, areg: lambda: norm_fin(
                        hs, rr, aslice, areg))(hs, rr, aslice, areg))
        for nf in pending_norm:
            nf()
        drip(len(pend))
        pend = make_tail(qc, par)
    drip(len(pend))

    b.barrier()
    cur[0] = FREE0
    w1h = [salloc(f"w1h{i}", [128, 8 * 1024], BF16) for i in range(2)]
    w2h = [salloc(f"w2h{i}", [128, 8 * 1024], BF16) for i in range(2)]
    ln2c = salloc("ln2c", [128, 8 * CC], BF16)
    acc = salloc("acc", [128, 8 * CC], F32)
    h_s = [salloc(f"h_s{i}", [128, TS], BF16) for i in range(2)]
    hs_s = [salloc(f"hs_s{i}", [128, TS], BF16) for i in range(2)]
    wb_s = [salloc(f"wb_s{i}", [128, TS], BF16) for i in range(4)]
    wrow8 = salloc("wrow8", [8, CC], F32)
    wrow8b = salloc("wrow8b", [8, CC], BF16)
    oneh_b = salloc("oneh_b", [8, 8 * 128], BF16)
    b1s = salloc("b1s", [128, 128], F32)
    b2s = salloc("b2s", [8, 1024], F32)
    gms = salloc("gms", [128, 8], F32)
    bts = salloc("bts", [128, 8], F32)
    _l2f = salloc("l2f", [128, 8 * TS], F32)
    _y1f = salloc("y1f", [128, 8 * TS], F32)
    l2f = [_l2f, _l2f]
    y1f = [_y1f, _y1f]
    y2wb = [salloc(f"y2w{i}", [128, 8 * TS], F32) for i in range(4)]
    tb0 = [salloc(f"tb0{i}", [128, TS], F32) for i in range(2)]
    tb1 = [salloc(f"tb1{i}", [128, TS], F32) for i in range(2)]
    otk = [salloc(f"otk{i}", [128, 1024], F32) for i in range(2)]
    sq3 = [salloc(f"sq3_{i}", [128, TS], F32) for i in range(2)]

    R_w1h = [Reg(), Reg()]
    R_w2h = [Reg(), Reg()]
    R_ln2c = Reg()
    R_acc = [Reg() for _ in range(8)]
    R_h = [Reg(), Reg()]
    R_hs = [Reg(), Reg()]
    R_wb = [Reg() for _ in range(4)]
    R_wrow8 = Reg()
    R_wrow8b = Reg()
    R_cst3 = Reg()
    _R_l2f = Reg()
    _R_y1f = Reg()
    R_l2f = [_R_l2f, _R_l2f]
    R_y1f = [_R_y1f, _R_y1f]
    R_y2wb = [[Reg() for _ in range(8)] for _ in range(4)]
    R_tb0 = [Reg(), Reg()]
    R_tb1 = [Reg(), Reg()]
    R_otk = [Reg(), Reg()]
    R_sq3 = [Reg(), Reg()]

    b.op("sp", lambda e: e.dma_start(b1s[:, :], b1r), wr=(R_cst3,))
    b.op("pool", lambda e: e.dma_start(oneh_b[:, :], onehd), wr=(R_cst3,))
    b.op("sp", lambda e: e.dma_start(b2s[:, :], b2d), wr=(R_cst3,))
    b.op("sp", lambda e: e.dma_start(gms[:, :], gamd), wr=(R_cst3,))
    b.op("sp", lambda e: e.dma_start(bts[:, :], betd), wr=(R_cst3,))

    NP = NE * 2
    all_l2sp = [R_l2sp[i][dc] for i in range(NT) for dc in range(8)]
    all_wtsp = list(R_wtsp)

    def w_load(p, buf):
        e_, fh = p // 2, p % 2
        b.op("pool", (lambda e_, fh, buf: lambda eng: eng.dma_start(
            w1h[buf].ap().rearrange("p (dc f) -> p dc f", dc=8),
            w1[e_, :, fh * 1024:(fh + 1) * 1024].rearrange(
                "(dc p) f -> p dc f", p=128)))(e_, fh, buf),
             wr=(R_w1h[buf],))
        b.op("pool", (lambda e_, fh, buf: lambda eng: eng.dma_start(
            w2h[buf].ap().rearrange("p (fc d) -> p fc d", fc=8),
            w2[e_, fh * 1024:(fh + 1) * 1024, :].rearrange(
                "(fc p) d -> p fc d", p=128)))(e_, fh, buf),
             wr=(R_w2h[buf],))

    w_load(0, 0)
    pend3 = []
    for cc in range(NCC):
        b.op("pool", (lambda cc: lambda e: e.dma_start(
            ln2c.ap().rearrange("p (dc t) -> p dc t", dc=8),
            ln2_sp[:, cc * CC:(cc + 1) * CC].rearrange(
                "(dc p) t -> p dc t", p=128)))(cc),
             rd=tuple(all_l2sp), wr=(R_ln2c,))
        b.op("sp", (lambda cc: lambda e: e.dma_start(
            wrow8[:, :], wt_sp[:, cc * CC:(cc + 1) * CC]))(cc),
             rd=tuple(all_wtsp), wr=(R_wrow8,))
        b.op("dve", lambda e: e.tensor_copy(wrow8b[:, :], wrow8[:, :]),
             rd=(R_wrow8,), wr=(R_wrow8b,))
        for p in range(NP):
            e_, fh = p // 2, p % 2
            buf = p % 2
            pass
            if p + 1 < NP:
                w_load(p + 1, (p + 1) % 2)
            elif cc + 1 < NCC:
                w_load(0, (p + 1) % 2)
            def wb_build(ts):
                if fh != 0:
                    return
                wbs = ts
                b.mmg([(lambda e_, ts: lambda e: e.matmul(
                    bank[6][:, (ts % 2) * 256:(ts % 2) * 256 + TS],
                    oneh_b[:, e_ * 128:(e_ + 1) * 128],
                    wrow8b[:, ts * TS:(ts + 1) * TS],
                    start=True, stop=True))(e_, ts)],
                      R_bank[6], [[R_cst3, R_wrow8b]])
                b.op("act", (lambda ts, wbs: lambda e: e.activation(
                    wb_s[wbs][:, :], bank[6][:, (ts % 2) * 256:(ts % 2) * 256 + TS],
                    AF.Identity))(ts, wbs),
                     rd=(R_bank[6],), wr=(R_wb[wbs],))

            def w1_stage(ts, fc):
                hb = (ts * 8 + fc) % 2
                mms = [(lambda dc, fc, buf, ts, hb: lambda e: e.matmul(
                    bank[4 + hb][:, 0:TS],
                    w1h[buf][:, dc * 1024 + fc * 128:dc * 1024 + (fc + 1) * 128],
                    ln2c[:, dc * CC + ts * TS:dc * CC + (ts + 1) * TS],
                    start=(dc == 0), stop=(dc == 7)))(dc, fc, buf, ts, hb)
                    for dc in range(8)]
                b.mmg(mms, R_bank[4 + hb],
                      [[R_w1h[buf], R_ln2c] for _ in range(8)])
                fcg = fh * 8 + fc
                b.op("act", (lambda hb, e_, fcg: lambda e: e.activation(
                    h_s[hb][:, :], bank[4 + hb][:, 0:TS], AF.Relu,
                    bias=b1s[:, e_ * 16 + fcg:e_ * 16 + fcg + 1]))(hb, e_, fcg),
                     rd=(R_bank[4 + hb], R_cst3), wr=(R_h[hb],))
                b.op("dve", (lambda hb, ts: lambda e: e.tensor_mul(
                    hs_s[hb][:, :], h_s[hb][:, :], wb_s[ts][:, :]))(hb, ts),
                     rd=(R_h[hb], R_wb[ts]), wr=(R_hs[hb],))

            def w2_stage(ts, fc):
                hb = (ts * 8 + fc) % 2
                eo_open = eo_open_by[ts]
                is_last_piece = (p == NP - 1)
                grp_start = (fc == 0)
                grp_end = (fc == 7)
                tok = None
                for dc in range(8):
                    bi, half = dc // 2, dc % 2
                    stop = grp_end and half == 1 and not is_last_piece
                    fn = (lambda dc, fc, buf, hb, bi, half, stop, st0, ts: lambda e: e.matmul(
                        bank[bi][:, half * 256:half * 256 + TS],
                        w2h[buf][:, fc * 1024 + dc * 128:fc * 1024 + (dc + 1) * 128],
                        hs_s[hb][:, :],
                        start=st0, stop=stop))(dc, fc, buf, hb, bi, half, stop,
                                               grp_start and half == 0, ts)
                    w = []
                    if grp_start and not eo_open[bi]:
                        w += R_bank[bi].wr()
                        eo_open[bi] = True
                    w += R_w2h[buf].rd() + R_hs[hb].rd()
                    inc = "pe" if dc == 7 else None
                    tok = b.emit("pe", fn, w, inc)
                R_w2h[buf].addr(tok)
                R_hs[hb].addr(tok)
                if grp_end and not is_last_piece:
                    for bi in range(4):
                        R_bank[bi].setw(tok)

            def eo_evac(ts):
                if p == NP - 1:
                    mms = []
                    for dc in range(8):
                        bi, half = dc // 2, dc % 2
                        mms.append((lambda dc, ts, bi, half: lambda e: e.matmul(
                            bank[bi][:, half * 256:half * 256 + TS],
                            b2s[:, dc * 128:(dc + 1) * 128],
                            wrow8[:, ts * TS:(ts + 1) * TS],
                            start=False, stop=(half == 1)))(dc, ts, bi, half))
                    tok = None
                    for i, fn in enumerate(mms):
                        w = R_cst3.rd() + R_wrow8.rd()
                        tok = b.emit("pe", fn, w, "pe" if i == 7 else None)
                    R_cst3.addr(tok)
                    R_wrow8.addr(tok)
                    for bi in range(4):
                        R_bank[bi].setw(tok)
                for bi in range(4):
                    src_ap = bass.AP(bank[bi], 0, [[512, 128], [256, 2], [1, TS]])
                    dst_ap = bass.AP(acc, 2 * bi * CC + ts * TS,
                                     [[8 * CC, 128], [CC, 2], [1, TS]])
                    if p == 0:
                        fn = (lambda d, s: lambda e: e.tensor_copy(d, s))(dst_ap, src_ap)
                    else:
                        fn = (lambda d, s: lambda e: e.tensor_add(d, s, d))(dst_ap, src_ap)
                    b.op("dve", fn, rd=(R_bank[bi],),
                         wr=(R_acc[2 * bi], R_acc[2 * bi + 1]))

            eo_open_by = {ts: [False] * 4 for ts in range(NTS)}
            stages = [(ts, fc) for ts in range(NTS) for fc in range(8)]
            wb_build(0)
            w1_stage(*stages[0])
            for i in range(len(stages)):
                nts_, nfc_ = stages[i + 1] if i + 1 < len(stages) else (None, None)
                if nts_ is not None:
                    if nfc_ == 0:
                        wb_build(nts_)
                    w1_stage(nts_, nfc_)
                w2_stage(*stages[i])
                if stages[i][1] == 7:
                    eo_evac(stages[i][0])

        def ln3_a(cc, ts):
            tsb = ts % 2
            t0 = cc * CC + ts * TS
            y2w = y2wb[ts]
            R_y2w = R_y2wb[ts]
            b.op("sp", (lambda t0, tsb: lambda e: e.dma_start(
                l2f[tsb].ap().rearrange("p (dc t) -> p dc t", dc=8),
                ln2_sp[:, t0:t0 + TS].rearrange("(dc p) t -> p dc t", p=128)))(t0, tsb),
                 rd=tuple(all_l2sp), wr=(R_l2f[tsb],))
            for dc in range(8):
                b.op("dve", (lambda dc, ts, tsb, cc, y2w: lambda e: e.tensor_add(
                    y2w[:, dc * TS:(dc + 1) * TS],
                    acc[:, dc * CC + ts * TS:dc * CC + (ts + 1) * TS],
                    l2f[tsb][:, dc * TS:(dc + 1) * TS]))(dc, ts, tsb, cc, y2w),
                     rd=(R_acc[dc], R_l2f[tsb]), wr=(R_y2w[dc],))

        def ln3_b1(cc, ts):
            bk7 = bank[6 + ts % 2]
            R_bk7 = R_bank[6 + ts % 2]
            tsb = ts % 2
            t0 = cc * CC + ts * TS
            y2w = y2wb[ts]
            R_y2w = R_y2wb[ts]
            mms = [(lambda dc, y2w: lambda e: e.matmul(
                bk7[0:1, 0:TS], ones_c.ap(), y2w[:, dc * TS:(dc + 1) * TS],
                start=(dc == 0), stop=(dc == 7)))(dc, y2w) for dc in range(8)]
            b.mmg(mms, R_bk7, [[R_const, R_y2w[dc]] for dc in range(8)])
            for dc in range(8):
                sl = dc % 2
                b.op("dve", (lambda dc, sl, y2w: lambda e: e.tensor_mul(
                    sq3[sl][:, :], y2w[:, dc * TS:(dc + 1) * TS],
                    y2w[:, dc * TS:(dc + 1) * TS]))(dc, sl, y2w),
                     rd=(R_y2w[dc],), wr=(R_sq3[sl],))
                fn = (lambda dc, sl: lambda e: e.matmul(
                    bk7[32:33, 0:TS], ones_c.ap(), sq3[sl][:, :],
                    start=(dc == 0), stop=(dc == 7)))(dc, sl)
                w = (R_bk7.wr() if dc == 0 else []) + R_const.rd() + R_sq3[sl].rd()
                tok = b.emit("pe", fn, w, "pe")
                R_const.addr(tok)
                R_sq3[sl].addr(tok)
                if dc == 7:
                    R_bk7.setw(tok)

        def ln3_b2(cc, ts):
            bk7 = bank[6 + ts % 2]
            R_bk7 = R_bank[6 + ts % 2]
            tsb = ts % 2
            y2w = y2wb[ts]
            R_y2w = R_y2wb[ts]
            rM = rows[0:1, 0 * T:0 * T + TS]
            rV = rows[0:1, 1 * T:1 * T + TS]
            rA = rows[0:1, 2 * T:2 * T + TS]
            rB = rows[0:1, 3 * T:3 * T + TS]
            rT_ = rows[0:1, 4 * T:4 * T + TS]
            rL = rT_
            b.op("dve", lambda e: e.tensor_scalar_mul(rM, bk7[0:1, 0:TS], 1.0 / D),
                 rd=(R_bk7,), wr=(R_rows,))
            b.op("dve", lambda e: e.tensor_scalar_mul(rV, bk7[32:33, 0:TS], 1.0 / D),
                 rd=(R_bk7,), wr=(R_rows,))
            b.op("dve", lambda e: e.tensor_mul(rT_, rM, rM), rd=(R_rows,), wr=(R_rows,))
            b.op("dve", lambda e: e.tensor_sub(rV, rV, rT_), rd=(R_rows,), wr=(R_rows,))
            b.op("act", lambda e: e.activation(rL, rV, AF.Ln, bias=epss[0:1, 0:1]),
                 rd=(R_rows,), wr=(R_rows,))
            b.op("act", lambda e: e.activation(rA, rL, AF.Exp, scale=-0.5),
                 rd=(R_rows,), wr=(R_rows,))
            b.op("dve", lambda e: e.scalar_tensor_tensor(rB, rM, -1.0, rA,
                                                         ALU.mult, ALU.mult),
                 rd=(R_rows,), wr=(R_rows,))
            b.mmg([lambda e: e.matmul(bk7[:, 0:TS], ones_r[0:1, 0:128], rA,
                                      start=True, stop=True)],
                  R_bk7, [[R_const, R_rows]])
            b.mmg([lambda e: e.matmul(bk7[:, 256:256 + TS], ones_r[0:1, 0:128], rB,
                                      start=True, stop=True)],
                  R_bk7, [[R_const, R_rows]])
            for dc in range(8):
                sl = dc % 2
                b.op("dve", (lambda dc, sl, y2w: lambda e: e.tensor_mul(
                    tb0[sl][:, :], y2w[:, dc * TS:(dc + 1) * TS],
                    bk7[:, 0:TS]))(dc, sl, y2w),
                     rd=(R_y2w[dc], R_bk7), wr=(R_tb0[sl],))
                b.op("dve", (lambda dc, sl: lambda e: e.tensor_add(
                    tb1[sl][:, :], tb0[sl][:, :], bk7[:, 256:256 + TS]))(dc, sl),
                     rd=(R_tb0[sl], R_bk7), wr=(R_tb1[sl],))
                b.op("act", (lambda dc, sl, y2w: lambda e: e.activation(
                    y2w[:, dc * TS:(dc + 1) * TS], tb1[sl][:, :], AF.Identity,
                    scale=gms[:, dc:dc + 1], bias=bts[:, dc:dc + 1]))(dc, sl, y2w),
                     rd=(R_tb1[sl], R_cst3), wr=(R_y2w[dc],))

        def ln3_b3(cc, ts):
            tsb = 0
            t0 = cc * CC + ts * TS
            y2w = y2wb[ts]
            R_y2w = R_y2wb[ts]
            b.op("sp", (lambda t0: lambda e: e.dma_start(
                y1f[0].ap().rearrange("p (dc t) -> p dc t", dc=8),
                y1_sp[:, t0:t0 + TS].rearrange("(dc p) t -> p dc t", p=128)))(t0),
                 rd=[r for qq in range(NT) for r in R_y1sp[qq]], wr=(R_y1f[0],))
            for dc in range(8):
                b.op("dve", (lambda dc, y2w: lambda e: e.tensor_add(
                    y2w[:, dc * TS:(dc + 1) * TS], y2w[:, dc * TS:(dc + 1) * TS],
                    y1f[0][:, dc * TS:(dc + 1) * TS]))(dc, y2w),
                     rd=(R_y1f[0],), wr=(R_y2w[dc],))
            for tt2 in range(2):
                ob = tt2 % 2
                for dc in range(8):
                    tb = dc % 4
                    b.mmg([(lambda dc, tt2, tb, y2w: lambda e: e.transpose(
                        bank[tb][:, 0:128],
                        y2w[:, dc * TS + tt2 * 128:dc * TS + (tt2 + 1) * 128],
                        id_f[:, :]))(dc, tt2, tb, y2w)],
                          R_bank[tb], [[R_y2w[dc], R_const]])
                    b.op("act", (lambda dc, ob, tb: lambda e: e.activation(
                        otk[ob][:, dc * 128:(dc + 1) * 128],
                        bank[tb][:, 0:128], AF.Identity))(dc, ob, tb),
                         rd=(R_bank[tb],), wr=(R_otk[ob],))
                trow = t0 + tt2 * 128
                b.op("sp", (lambda trow, ob: lambda e: e.dma_start(
                    out_d[trow:trow + 128, :], otk[ob][:, :]))(trow, ob),
                     rd=(R_otk[ob],))

        for ts in range(NTS):
            ln3_a(cc, ts)
        for k in range(NTS + 2):
            if k < NTS:
                ln3_b1(cc, k)
            if 1 <= k <= NTS:
                ln3_b2(cc, k - 1)
            if k >= 2:
                ln3_b3(cc, k - 2)

    b.emit("sp", None, [(s, b.counts[s]) for s in DMASEMS], None)

    with nc.Block() as block:
        def runner(eng_name):
            def run(engobj):
                incsz = {s: 16 for s in DMASEMS}
                for ws, fn, inc in b.stream[eng_name]:
                    for s, v in ws:
                        engobj.wait_ge(sems[s], v)
                    if fn is not None:
                        ins = fn(engobj)
                        if inc is not None:
                            ins.then_inc(sems[inc], incsz.get(inc, 1))
            return run
        block.tensor(runner("pe"))
        block.scalar(runner("act"))
        block.vector(runner("dve"))
        block.sync(runner("sp"))
        block.gpsimd(runner("pool"))
    return nc


def _consts(S):
    pos = np.arange(S, dtype=np.float64)
    j = np.arange(DK // 2, dtype=np.float64)
    rates = 10000.0 ** (-(2.0 * j) / DK)
    ang = pos[None, :] * rates[:, None]
    cos_t = np.cos(ang)
    sin_t = np.sin(ang)
    ctab = np.zeros((128, S), np.float32)
    stab = np.zeros((128, S), np.float32)
    for r in range(128):
        lj = (r % 64) // 2
        ctab[r] = cos_t[lj]
        stab[r] = (-sin_t[lj]) if (r % 64) % 2 == 0 else sin_t[lj]
    p2 = np.zeros((128, 128), np.float32)
    for m in range(128):
        p2[m ^ 1, m] = 1.0
    ident = np.eye(128, dtype=np.float32)
    masks = np.zeros((128, 4 * T), np.float32)
    pk = np.arange(128)[:, None]
    pq = np.arange(T)[None, :]
    for r in range(4):
        masks[:, r * T:(r + 1) * T] = (pk + 128 * r <= pq).astype(np.float32)
    return ctab, stab, p2, ident, masks


_NC_CACHE = {}


def kernel(L0_attn_x, L0_attn_Wq, L0_attn_Wk, L0_attn_Wv, L0_attn_Wo,
           L0_ffn_Wg, L0_ffn_bg, L0_ffn_W1, L0_ffn_b1, L0_ffn_W2, L0_ffn_b2,
           L0_ffn_gamma, L0_ffn_beta):
    x = np.asarray(L0_attn_x, np.float32)
    B, S, _ = x.shape
    assert B == 8 and _ == D
    ctab, stab, p2, ident, masks = _consts(S)
    common = dict(
        wqt=np.ascontiguousarray(np.asarray(L0_attn_Wq, np.float32).T),
        wkt=np.ascontiguousarray(np.asarray(L0_attn_Wk, np.float32).T),
        wvt=np.ascontiguousarray(np.asarray(L0_attn_Wv, np.float32).T),
        wot=np.ascontiguousarray(np.asarray(L0_attn_Wo, np.float32).T),
        wg=np.ascontiguousarray(np.asarray(L0_ffn_Wg, np.float32)),
        bgd=np.asarray(L0_ffn_bg, np.float32).reshape(1, NE),
        w1=np.ascontiguousarray(np.asarray(L0_ffn_W1, np.float32)),
        b1r=np.ascontiguousarray(
            np.asarray(L0_ffn_b1, np.float32).reshape(NE, 16, 128)
            .transpose(2, 0, 1).reshape(128, 128)),
        w2=np.ascontiguousarray(np.asarray(L0_ffn_W2, np.float32)),
        b2d=np.ascontiguousarray(np.asarray(L0_ffn_b2, np.float32)),
        gamd=np.ascontiguousarray(
            np.asarray(L0_ffn_gamma, np.float32).reshape(8, 128).T),
        betd=np.ascontiguousarray(
            np.asarray(L0_ffn_beta, np.float32).reshape(8, 128).T),
        ctabd=ctab, stabd=stab, p2d=p2, idd=ident, maskd=masks,
        onesd=np.ones((1, 128), np.float32),
        onescd=np.ones((128, 1), np.float32),
        epsd=np.full((1, 1), EPS, np.float32),
        onehd=np.ascontiguousarray(np.kron(np.eye(8, dtype=np.float32),
                                           np.ones((1, 128), np.float32))),
    )
    key = S
    if key not in _NC_CACHE:
        _NC_CACHE[key] = build(S)
    nc = _NC_CACHE[key]
    in_maps = []
    for c in range(8):
        m = dict(common)
        m["xT"] = np.ascontiguousarray(x[c].T)
        in_maps.append(m)
    res = run_bass_kernel_spmd(nc, in_maps, core_ids=list(range(8)))
    return np.stack([r["out"] for r in res.results], axis=0)


# version 27
# speedup vs baseline: 1.0147x; 1.0066x over previous
import numpy as np
import ml_dtypes
import concourse.bass as bass
import concourse.mybir as mybir
from concourse.bass_utils import run_bass_kernel_spmd

F32 = mybir.dt.float32
BF16 = mybir.dt.bfloat16
AF = mybir.ActivationFunctionType
ALU = mybir.AluOpType
AX = mybir.AxisListType.X
NPBF = ml_dtypes.bfloat16

DEBUG = False
D = 1024
H = 16
DK = 64
NE = 8
FF = 2048
T = 512
TS = 256
EPS = 1e-5

ENGS = ("pe", "act", "dve", "sp", "pool")
DMAQ = {"sp": ("dh0", "dh1"), "pool": ("ds0", "ds1")}
DMASEMS = ("dh0", "dh1", "ds0", "ds1")
CSEM = {"pe": "pe", "act": "act", "dve": "dve", "sp": "spc", "pool": "poolc"}
ALLSEMS = ("pe", "act", "dve", "spc", "poolc", "dh0", "dh1", "ds0", "ds1")


class Reg:
    __slots__ = ("w", "rs")

    def __init__(self):
        self.w = None
        self.rs = []

    def rd(self):
        return [self.w] if self.w is not None else []

    def wr(self):
        out = list(self.rs)
        if self.w is not None:
            out.append(self.w)
        return out

    def setw(self, tok):
        if tok is not None:
            self.w = tok
            self.rs = []

    def addr(self, tok):
        if tok is not None:
            self.rs.append(tok)


class Bld:
    def __init__(self):
        self.counts = {s: 0 for s in ALLSEMS}
        self.stream = {e: [] for e in ENGS}
        self.wm = {e: {s: 0 for s in ALLSEMS} for e in ENGS}
        self.dmaq_idx = {"sp": 0, "pool": 0}

    def emit(self, eng, fn, waits, inc):
        ws = {}
        for t in waits:
            if t is None:
                continue
            s, v = t
            if v > self.wm[eng][s] and v > ws.get(s, 0):
                ws[s] = v
        for s, v in ws.items():
            self.wm[eng][s] = v
        tok = None
        if inc is not None:
            n = 16 if inc in DMASEMS else 1
            self.counts[inc] += n
            tok = (inc, self.counts[inc])
        self.stream[eng].append((tuple(sorted(ws.items())), fn, inc))
        return tok

    def op(self, eng, fn, rd=(), wr=(), sem=0, extra=()):
        waits = list(extra)
        if sem == 0:
            if eng in DMAQ:
                i = self.dmaq_idx[eng]
                self.dmaq_idx[eng] = 1 - i
                sem = DMAQ[eng][i]
                waits.append((sem, self.counts[sem]))
            else:
                sem = CSEM[eng]
        for r in rd:
            waits += r.rd()
        for r in wr:
            waits += r.wr()
        tok = self.emit(eng, fn, waits, sem)
        for r in rd:
            r.addr(tok)
        for r in wr:
            r.setw(tok)
        return tok

    def mmg(self, mms, wr_reg, rd_each):
        n = len(mms)
        all_rd = []
        tok = None
        for i, (fn, rds) in enumerate(zip(mms, rd_each)):
            waits = list(wr_reg.wr()) if i == 0 else []
            for r in rds:
                waits += r.rd()
                all_rd.append(r)
            tok = self.emit("pe", fn, waits, "pe" if i == n - 1 else None)
        for r in all_rd:
            r.addr(tok)
        wr_reg.setw(tok)
        return tok

    def barrier(self):
        for e in ENGS:
            self.emit(e, lambda eng: eng.nop(),
                      [(CSEM[e], self.counts[CSEM[e]])], CSEM[e])
        snap = dict(self.counts)
        waits = [(s, snap[s]) for s in ALLSEMS]
        for e in ENGS:
            self.emit(e, None, waits, None)


def build(S):
    assert S % T == 0
    NT = S // T
    CC = 1024 if S % 1024 == 0 else S
    NCC = S // CC
    NTS = CC // TS
    NKT = S // 128

    nc = bass.Bass("TRN2", target_bir_lowering=False)

    xT = nc.dram_tensor("xT", [D, S], F32, kind="ExternalInput").ap()
    wqt = nc.dram_tensor("wqt", [D, D], F32, kind="ExternalInput").ap()
    wkt = nc.dram_tensor("wkt", [D, D], F32, kind="ExternalInput").ap()
    wvt = nc.dram_tensor("wvt", [D, D], F32, kind="ExternalInput").ap()
    wot = nc.dram_tensor("wot", [D, D], F32, kind="ExternalInput").ap()
    wg = nc.dram_tensor("wg", [D, NE], F32, kind="ExternalInput").ap()
    bgd = nc.dram_tensor("bgd", [1, NE], F32, kind="ExternalInput").ap()
    w1 = nc.dram_tensor("w1", [NE, D, FF], F32, kind="ExternalInput").ap()
    b1r = nc.dram_tensor("b1r", [128, 128], F32, kind="ExternalInput").ap()
    w2 = nc.dram_tensor("w2", [NE, FF, D], F32, kind="ExternalInput").ap()
    b2d = nc.dram_tensor("b2d", [NE, D], F32, kind="ExternalInput").ap()
    gamd = nc.dram_tensor("gamd", [128, 8], F32, kind="ExternalInput").ap()
    betd = nc.dram_tensor("betd", [128, 8], F32, kind="ExternalInput").ap()
    ctabd = nc.dram_tensor("ctabd", [128, S], F32, kind="ExternalInput").ap()
    stabd = nc.dram_tensor("stabd", [128, S], F32, kind="ExternalInput").ap()
    p2d = nc.dram_tensor("p2d", [128, 128], F32, kind="ExternalInput").ap()
    idd = nc.dram_tensor("idd", [128, 128], F32, kind="ExternalInput").ap()
    maskd = nc.dram_tensor("maskd", [128, 4 * T], F32, kind="ExternalInput").ap()
    onesd = nc.dram_tensor("onesd", [1, 128], F32, kind="ExternalInput").ap()
    onescd = nc.dram_tensor("onescd", [128, 1], F32, kind="ExternalInput").ap()
    epsd = nc.dram_tensor("epsd", [1, 1], F32, kind="ExternalInput").ap()
    onehd = nc.dram_tensor("onehd", [8, 8 * 128], F32, kind="ExternalInput").ap()
    out_d = nc.dram_tensor("out", [S, D], F32, kind="ExternalOutput").ap()

    if DEBUG:
        dbg_ln1 = nc.dram_tensor("dbg_ln1", [D, S], BF16, kind="Internal").ap()
        dbg_rows = nc.dram_tensor("dbg_rows", [S // T, 6 * T], F32, kind="Internal").ap()
    q_sp = nc.dram_tensor("q_sp", [D, S], BF16, kind="Internal").ap()
    k_sp = nc.dram_tensor("k_sp", [D, S], BF16, kind="Internal").ap()
    v_sp = nc.dram_tensor("v_sp", [NKT, 128, H * 65], BF16, kind="Internal").ap()
    ln2_sp = nc.dram_tensor("ln2_sp", [D, S], F32, kind="Internal").ap()
    y1_sp = nc.dram_tensor("y1_sp", [D, S], F32, kind="Internal").ap()
    wt_sp = nc.dram_tensor("wt_sp", [NE, S], F32, kind="Internal").ap()

    b = Bld()

    ones_r = nc.alloc_sbuf_tensor("ones_r", [1, 128], F32)
    ones_b = nc.alloc_sbuf_tensor("ones_b", [1, 128], BF16)
    ones_c = nc.alloc_sbuf_tensor("ones_c", [128, 1], F32)
    id_f = nc.alloc_sbuf_tensor("id_f", [128, 128], F32)
    rows = nc.alloc_sbuf_tensor("rows", [1, 5 * T], F32)
    epss = nc.alloc_sbuf_tensor("epss", [1, 1], F32)
    bank = [nc.alloc_psum_tensor(f"bank{i}", [128, 512], F32) for i in range(8)]
    sems = {s: nc.alloc_semaphore(s) for s in ALLSEMS}

    _fence = nc.alloc_sbuf_tensor("fence", [1, 8], F32)
    FREE0 = (nc.lookup_mloc(_fence).addr + 63) & ~31
    cur = [FREE0]

    def salloc(name, shape, dt):
        width = int(np.prod(shape[1:]))
        nbytes = (width * (2 if dt == BF16 else 4) + 31) & ~31
        off = cur[0]
        cur[0] += nbytes
        assert cur[0] <= 223 * 1024, f"SBUF overflow at {name}: {cur[0] // 1024} KiB"
        return nc.alloc_sbuf_tensor_at(name, list(shape), dt, offset=off)

    R_const = Reg()
    R_rows = Reg()
    R_bank = [Reg() for _ in range(8)]

    def emit_ln(get_x, x_regs, width, sqB, R_sqB, bs=2, ba=3):
        bk2, bk3 = bank[bs], bank[ba]
        x_ap = [get_x(dc) for dc in range(8)]
        mms = [(lambda xa, dc: lambda e: e.matmul(
            bk2[0:1, 0:width], ones_c.ap(), xa,
            start=(dc == 0), stop=(dc == 7)))(x_ap[dc], dc) for dc in range(8)]
        b.mmg(mms, R_bank[bs], [[R_const, x_regs[dc]] for dc in range(8)])
        for dc in range(8):
            sl = dc % 2
            b.op("dve", (lambda xa, sl: lambda e: e.tensor_mul(
                sqB[sl][:, 0:width], xa, xa))(x_ap[dc], sl),
                 rd=(x_regs[dc],), wr=(R_sqB[sl],))
            fn = (lambda dc, sl: lambda e: e.matmul(
                bk2[32:33, 0:width], ones_c.ap(), sqB[sl][:, 0:width],
                start=(dc == 0), stop=(dc == 7)))(dc, sl)
            w = (R_bank[bs].wr() if dc == 0 else []) + R_const.rd() + R_sqB[sl].rd()
            tok = b.emit("pe", fn, w, "pe")
            R_const.addr(tok)
            R_sqB[sl].addr(tok)
            if dc == 7:
                R_bank[bs].setw(tok)
        rM = rows[0:1, 0 * T:0 * T + width]
        rV = rows[0:1, 1 * T:1 * T + width]
        rA = rows[0:1, 2 * T:2 * T + width]
        rB = rows[0:1, 3 * T:3 * T + width]
        rT_ = rows[0:1, 4 * T:4 * T + width]
        rL = rT_
        b.op("dve", lambda e: e.tensor_scalar_mul(rM, bank[bs][0:1, 0:width], 1.0 / D),
             rd=(R_bank[bs],), wr=(R_rows,))
        b.op("dve", lambda e: e.tensor_scalar_mul(rV, bank[bs][32:33, 0:width], 1.0 / D),
             rd=(R_bank[bs],), wr=(R_rows,))
        b.op("dve", lambda e: e.tensor_mul(rT_, rM, rM), rd=(R_rows,), wr=(R_rows,))
        b.op("dve", lambda e: e.tensor_sub(rV, rV, rT_), rd=(R_rows,), wr=(R_rows,))
        b.op("act", lambda e: e.activation(rL, rV, AF.Ln, bias=epss[0:1, 0:1]),
             rd=(R_rows,), wr=(R_rows,))
        b.op("act", lambda e: e.activation(rA, rL, AF.Exp, scale=-0.5),
             rd=(R_rows,), wr=(R_rows,))
        b.op("dve", lambda e: e.scalar_tensor_tensor(rB, rM, -1.0, rA,
                                                     ALU.mult, ALU.mult),
             rd=(R_rows,), wr=(R_rows,))
        b.mmg([lambda e: e.matmul(bk3[:, 0:width], ones_r[0:1, 0:128], rA,
                                  start=True, stop=True)],
              R_bank[ba], [[R_const, R_rows]])
        b.mmg([lambda e: e.matmul(bk2[:, 0:width], ones_r[0:1, 0:128], rB,
                                  start=True, stop=True)],
              R_bank[bs], [[R_const, R_rows]])

    cs = salloc("cs_tab", [128, S], F32)
    ss = salloc("ss_tab", [128, S], F32)
    p2s = salloc("p2s", [128, 128], F32)
    wq_s = salloc("wq_s", [128, 8 * 1024], BF16)
    wk_s = salloc("wk_s", [128, 8 * 1024], BF16)
    wv_s = salloc("wv_s", [128, 8 * 1024], BF16)
    xin = [salloc(f"xin{i}", [128, 8 * T], F32) for i in range(2)]
    ln1 = salloc("ln1", [128, 8 * T], BF16)
    sq1 = [salloc(f"sq1_{i}", [128, T], F32) for i in range(2)]
    qraw = [salloc(f"qraw{i}", [128, T], F32) for i in range(2)]
    tm0 = [salloc(f"tm0{i}", [128, T], F32) for i in range(2)]
    tm1 = [salloc(f"tm1{i}", [128, T], F32) for i in range(2)]
    qst = salloc("qst", [128, 8 * T], BF16)
    kst = salloc("kst", [128, 8 * T], BF16)
    vfm = [salloc(f"vfm{i}", [128, T], F32) for i in range(2)]
    vst = salloc("vst", [128, 4 * 1040], BF16)

    R_xin = [Reg(), Reg()]
    R_ln1 = [Reg() for _ in range(8)]
    R_sq1 = [Reg(), Reg()]
    R_qraw = [Reg(), Reg()]
    R_tm0 = [Reg(), Reg()]
    R_tm1 = [Reg(), Reg()]
    R_qst = [Reg() for _ in range(8)]
    R_kst = [Reg() for _ in range(8)]
    R_vfm = [Reg(), Reg()]
    R_vst = [Reg() for _ in range(4)]
    R_qsp = [Reg() for _ in range(NT)]
    R_ksp = [Reg() for _ in range(NT)]
    R_vsp = [Reg() for _ in range(NT)]

    b.op("dve", lambda e: e.memset(vst[:, :], 1.0), wr=tuple(R_vst))

    b.op("pool", lambda e: e.dma_start(ones_b[:, :], onesd), wr=(R_const,))
    R_warm = Reg()
    b.op("dve", lambda e: e.memset(kst[0:1, 0:512], 0.0), wr=(R_warm,))
    for wi_ in range(64):
        b.op("pe", (lambda: lambda e: e.matmul(
            bank[7][:, :], ones_b[0:1, 0:128], kst[0:1, 0:512],
            start=True, stop=True))(),
             rd=(R_const, R_warm))
    for dst, src in ((ones_r, onesd), (ones_c, onescd), (id_f, idd),
                     (cs, ctabd), (ss, stabd), (p2s, p2d), (epss, epsd)):
        b.op("sp", (lambda d, s: lambda e: e.dma_start(d[:, :], s))(dst, src),
             wr=(R_const,))
    for dst, src in ((wq_s, wqt), (wk_s, wkt), (wv_s, wvt)):
        b.op("pool", (lambda d, s: lambda e: e.dma_start(
            d.ap().rearrange("p (kc d) -> p kc d", kc=8),
            s.rearrange("(kc p) d -> p kc d", p=128)))(dst, src),
             wr=(R_const,))

    for tc in range(NT):
        xb = tc % 2
        b.op("sp", (lambda tc, xb: lambda e: e.dma_start(
            xin[xb].ap().rearrange("p (dc t) -> p dc t", dc=8),
            xT[:, tc * T:(tc + 1) * T].rearrange("(dc p) t -> p dc t", p=128)))(tc, xb),
             wr=(R_xin[xb],))
        emit_ln(lambda dc: xin[xb][:, dc * T:(dc + 1) * T], [R_xin[xb]] * 8,
                T, sq1, R_sq1)
        if DEBUG:
            b.op("sp", (lambda tc: lambda e: e.dma_start(
                dbg_rows[tc:tc + 1, :], rows[0:1, :]))(tc), rd=(R_rows,))
        for dc in range(8):
            sl = dc % 2
            b.op("dve", (lambda dc, sl, xb: lambda e: e.tensor_mul(
                tm0[sl][:, :], xin[xb][:, dc * T:(dc + 1) * T], bank[3][:, :]))(dc, sl, xb),
                 rd=(R_xin[xb], R_bank[3]), wr=(R_tm0[sl],))
            b.op("dve", (lambda dc, sl: lambda e: e.tensor_add(
                ln1[:, dc * T:(dc + 1) * T], tm0[sl][:, :], bank[2][:, :]))(dc, sl),
                 rd=(R_tm0[sl], R_bank[2]), wr=(R_ln1[dc],))

        def proj_group(wi, dc):
            wsb = (wq_s, wk_s, wv_s)[wi]
            g = (wi * 8 + dc) % 2
            mms = [(lambda kc, dc, wsb, g: lambda e: e.matmul(
                bank[g][:, :],
                wsb[:, kc * 1024 + dc * 128:kc * 1024 + (dc + 1) * 128],
                ln1[:, kc * T:(kc + 1) * T],
                start=(kc == 0), stop=(kc == 7)))(kc, dc, wsb, g)
                for kc in range(8)]
            b.mmg(mms, R_bank[g], [[R_const, R_ln1[kc]] for kc in range(8)])

        def proj_post(wi, dc, tc):
            g = (wi * 8 + dc) % 2
            if wi < 2:
                qs = dc % 2
                b.op("act", (lambda qs, g: lambda e: e.activation(
                    qraw[qs][:, :], bank[g][:, :], AF.Identity))(qs, g),
                     rd=(R_bank[g],), wr=(R_qraw[qs],))
                sb_i = 5 + (dc % 2)
                b.mmg([(lambda qs, sb_i: lambda e: e.matmul(
                    bank[sb_i][:, :], p2s[:, :], qraw[qs][:, :],
                    start=True, stop=True))(qs, sb_i)],
                      R_bank[sb_i], [[R_const, R_qraw[qs]]])
                b.op("dve", (lambda qs, tc: lambda e: e.tensor_mul(
                    tm0[qs][:, :], qraw[qs][:, :], cs[:, tc * T:(tc + 1) * T]))(qs, tc),
                     rd=(R_qraw[qs], R_const), wr=(R_tm0[qs],))
                b.op("dve", (lambda qs, sb_i, tc: lambda e: e.tensor_mul(
                    tm1[qs][:, :], bank[sb_i][:, :], ss[:, tc * T:(tc + 1) * T]))(qs, sb_i, tc),
                     rd=(R_bank[sb_i], R_const), wr=(R_tm1[qs],))
                stage = qst if wi == 0 else kst
                sreg = (R_qst if wi == 0 else R_kst)[dc]
                b.op("dve", (lambda qs, dc, stage: lambda e: e.tensor_add(
                    stage[:, dc * T:(dc + 1) * T], tm0[qs][:, :], tm1[qs][:, :]))(qs, dc, stage),
                     rd=(R_tm0[qs], R_tm1[qs]), wr=(sreg,))
            else:
                vs = dc % 2
                b.op("act", (lambda vs, g: lambda e: e.activation(
                    vfm[vs][:, :], bank[g][:, :], AF.Identity))(vs, g),
                     rd=(R_bank[g],), wr=(R_vfm[vs],))
                for tt in range(4):
                    tb = 4 if (dc * 4 + tt) % 2 == 0 else 7
                    b.mmg([(lambda vs, tt, tb: lambda e: e.transpose(
                        bank[tb][:, 0:128],
                        vfm[vs][:, tt * 128:(tt + 1) * 128], id_f[:, :]))(vs, tt, tb)],
                          R_bank[tb], [[R_vfm[vs], R_const]])
                    for hs2 in range(2):
                        hcol = 65 * (2 * dc + hs2)
                        b.op("dve", (lambda tt, tb, hs2, hcol: lambda e: e.tensor_copy(
                            vst[:, tt * 1040 + hcol:tt * 1040 + hcol + 64],
                            bank[tb][:, hs2 * 64:hs2 * 64 + 64]))(tt, tb, hs2, hcol),
                             rd=(R_bank[tb],), wr=(R_vst[tt],))

        items = [(wi, dc) for wi in range(3) for dc in range(8)]
        proj_group(*items[0])
        for i in range(24):
            if i + 1 < 24:
                proj_group(*items[i + 1])
            proj_post(*items[i], tc)
        b.op("sp", (lambda tc: lambda e: e.dma_start(
            q_sp[:, tc * T:(tc + 1) * T].rearrange("(dc p) t -> p dc t", p=128),
            qst.ap().rearrange("p (dc t) -> p dc t", dc=8)))(tc),
             rd=tuple(R_qst), wr=(R_qsp[tc],))
        b.op("sp", (lambda tc: lambda e: e.dma_start(
            k_sp[:, tc * T:(tc + 1) * T].rearrange("(dc p) t -> p dc t", p=128),
            kst.ap().rearrange("p (dc t) -> p dc t", dc=8)))(tc),
             rd=tuple(R_kst), wr=(R_ksp[tc],))
        b.op("sp", (lambda tc: lambda e: e.dma_start(
            v_sp[tc * 4:(tc + 1) * 4].rearrange("tt p c -> p tt c"),
            vst.ap().rearrange("p (tt c) -> p tt c", tt=4)))(tc),
             rd=tuple(R_vst), wr=(R_vsp[tc],))

    b.barrier()
    cur[0] = FREE0
    kt_s = salloc("kt_s", [128, 8 * S], BF16)
    vh_s = salloc("vh_s", [128, NKT * 1040], BF16)
    qt_s = salloc("qt_s", [128, 8 * T], BF16)
    wo_dc = [salloc(f"wo_dc{i}", [128, 8 * 128], BF16) for i in range(2)]
    mk_s = salloc("mk_s", [128, 4 * T], BF16)
    pr_s = [salloc(f"pr{i}", [128, T], BF16) for i in range(3)]
    _xy1 = salloc("xy1", [128, 8 * T], F32)
    xy1b = [_xy1, _xy1]
    att_b = [salloc(f"att_s{i}", [128, 8 * T], BF16) for i in range(2)]
    rb_s = salloc("rb_s", [128, T], F32)
    rr_s = salloc("rr_s", [1, 2 * T], F32)
    rr_b = salloc("rr_b", [1, 2 * T], BF16)
    _sq2 = salloc("sq2", [128, T], F32)
    sq2 = [_sq2, _sq2]
    wgs = salloc("wgs", [128, 8 * 8], F32)
    bgs = salloc("bgs", [1, 8], F32)
    rt = salloc("rt", [128, 6 * 8], F32)
    rt1 = salloc("rt1", [128, 8], F32)
    wst = salloc("wst", [8, T], F32)

    R_kts = [Reg() for _ in range(8)]
    R_vhs = [Reg() for _ in range(NKT)]
    R_vones = Reg()
    R_qts = Reg()
    R_wodc = [Reg(), Reg()]
    R_mks = Reg()
    R_prs = [Reg() for _ in range(3)]
    _R_xy1 = [Reg() for _ in range(8)]
    R_xy1b = [_R_xy1, _R_xy1]
    R_attb = [[Reg() for _ in range(16)] for _ in range(2)]
    R_rbs = Reg()
    R_rr1 = Reg()
    R_rrs = [R_rr1, R_rr1]
    R_rrx = Reg()
    R_attf = [Reg(), Reg()]
    _R_sq2 = Reg()
    R_sq2 = [_R_sq2, _R_sq2]
    R_wgs = Reg()
    R_rt = [Reg() for _ in range(6)]
    R_rt1 = Reg()
    R_wst = [Reg() for _ in range(4)]
    R_l2sp = [[Reg() for _ in range(8)] for _ in range(NT)]
    R_y1sp = [[Reg() for _ in range(8)] for _ in range(NT)]
    R_wtsp = [Reg() for _ in range(NT)]

    for hp in range(8):
        b.op("sp", (lambda hp: lambda e: e.dma_start(
            kt_s[:, hp * S:(hp + 1) * S], k_sp[hp * 128:(hp + 1) * 128, :]))(hp),
             rd=tuple(R_ksp), wr=(R_kts[hp],))
    ng = (NKT + 7) // 8
    for g in range(ng):
        n = min(8, NKT - g * 8)
        b.op("pool", (lambda g, n: lambda e: e.dma_start(
            vh_s[:, g * 8 * 1040:(g * 8 + n) * 1040].rearrange(
                "p (kt c) -> p kt c", kt=n),
            v_sp[g * 8:g * 8 + n].rearrange("kt p c -> p kt c")))(g, n),
             rd=tuple(R_vsp), wr=tuple(R_vhs[g * 8:g * 8 + n]))
    b.op("pool", lambda e: e.dma_start(mk_s[:, :], maskd), wr=(R_mks,))
    b.op("sp", lambda e: e.dma_start(
        wgs.ap().rearrange("p (dc e8) -> p dc e8", dc=8),
        wg.rearrange("(dc p) e8 -> p dc e8", p=128)), wr=(R_wgs,))
    b.op("sp", lambda e: e.dma_start(bgs[:, :], bgd), wr=(R_wgs,))

    def make_tail(qc, par):
        xy1 = xy1b[par]
        att_s = att_b[par]
        R_xy1 = R_xy1b[par]
        R_att = R_attb[par]
        t = []

        def wo_load(dc):
            b.op("pool", (lambda dc: lambda e: e.dma_start(
                wo_dc[dc % 2].ap().rearrange("p (kc d) -> p kc d", kc=8),
                wot[:, dc * 128:(dc + 1) * 128].rearrange(
                    "(kc p) d -> p kc d", p=128)))(dc),
                 wr=(R_wodc[dc % 2],))

        def x_load():
            b.op("sp", (lambda qc: lambda e: e.dma_start(
                xy1.ap().rearrange("p (dc t) -> p dc t", dc=8),
                xT[:, qc * T:(qc + 1) * T].rearrange("(dc p) t -> p dc t", p=128)))(qc),
                 wr=tuple(R_xy1))
        t.append(lambda: (x_load(), wo_load(0), wo_load(1)))

        def wo_dc_step(dc):
            g = 6 + dc % 2
            mms = [(lambda kc, dc, g: lambda e: e.matmul(
                bank[g][:, :], wo_dc[dc % 2][:, kc * 128:(kc + 1) * 128],
                att_s[:, kc * T:(kc + 1) * T],
                start=(kc == 0), stop=(kc == 7)))(kc, dc, g) for kc in range(8)]
            b.mmg(mms, R_bank[g],
                  [[R_wodc[dc % 2], R_att[2 * kc], R_att[2 * kc + 1]] for kc in range(8)])
            if dc + 2 < 8:
                wo_load(dc + 2)
            b.op("dve", (lambda dc, g: lambda e: e.tensor_add(
                xy1[:, dc * T:(dc + 1) * T], bank[g][:, :],
                xy1[:, dc * T:(dc + 1) * T]))(dc, g),
                 rd=(R_bank[g],), wr=(R_xy1[dc],))
            b.op("sp", (lambda dc, qc: lambda e: e.dma_start(
                y1_sp[dc * 128:(dc + 1) * 128, qc * T:(qc + 1) * T],
                xy1[:, dc * T:(dc + 1) * T]))(dc, qc),
                 rd=(R_xy1[dc],), wr=(R_y1sp[qc][dc],))

        for dc in range(8):
            t.append((lambda dc: lambda: wo_dc_step(dc))(dc))

        def ln2_all():
            emit_ln(lambda dc: xy1[:, dc * T:(dc + 1) * T],
                    R_xy1, T, sq2, R_sq2, bs=6, ba=7)
            for dc in range(8):
                ln2_apply(dc)

        def ln2_apply(dc):
            b.op("dve", (lambda dc: lambda e: e.tensor_mul(
                xy1[:, dc * T:(dc + 1) * T], xy1[:, dc * T:(dc + 1) * T],
                bank[7][:, :]))(dc),
                 rd=(R_bank[7],), wr=(R_xy1[dc],),
                 extra=R_y1sp[qc][dc].rd())
            b.op("dve", (lambda dc: lambda e: e.tensor_add(
                xy1[:, dc * T:(dc + 1) * T], xy1[:, dc * T:(dc + 1) * T],
                bank[6][:, :]))(dc),
                 rd=(R_bank[6],), wr=(R_xy1[dc],))
            b.op("sp", (lambda dc, qc: lambda e: e.dma_start(
                ln2_sp[dc * 128:(dc + 1) * 128, qc * T:(qc + 1) * T],
                xy1[:, dc * T:(dc + 1) * T]))(dc, qc),
                 rd=(R_xy1[dc],), wr=(R_l2sp[qc][dc],))

        t.append(ln2_all)

        def router_tt(tt):
            mms = [(lambda dc, tt: lambda e: e.matmul(
                bank[6][0:128, 0:8],
                xy1[:, dc * T + tt * 128:dc * T + (tt + 1) * 128],
                wgs[:, dc * 8:(dc + 1) * 8],
                start=(dc == 0), stop=False))(dc, tt) for dc in range(8)]
            mms.append(lambda e: e.matmul(
                bank[6][0:128, 0:8], ones_r[0:1, 0:128], bgs[:, :],
                start=False, stop=True))
            b.mmg(mms, R_bank[6],
                  [[R_xy1[dc], R_wgs] for dc in range(8)] + [[R_const, R_wgs]])
            c = lambda i: rt[:, i * 8:(i + 1) * 8]
            v1c, v2c = rt1[:, 0:1], rt1[:, 1:2]
            w2c, w1c, tc1 = rt1[:, 2:3], rt1[:, 3:4], rt1[:, 4:5]
            b.op("dve", lambda e: e.reduce_max(v1c, bank[6][0:128, 0:8], axis=AX),
                 rd=(R_bank[6],), wr=(R_rt1,))
            b.op("dve", lambda e: e.tensor_scalar(c(0), bank[6][0:128, 0:8],
                                                  v1c, None, ALU.subtract),
                 rd=(R_bank[6], R_rt1), wr=(R_rt[0],))
            b.op("dve", lambda e: e.tensor_scalar(c(1), c(0), 0.0, None, ALU.is_equal),
                 rd=(R_rt[0],), wr=(R_rt[1],))
            b.op("dve", lambda e: e.scalar_tensor_tensor(
                c(2), c(1), -1e9, c(0), ALU.mult, ALU.add),
                 rd=(R_rt[0], R_rt[1]), wr=(R_rt[2],))
            b.op("dve", lambda e: e.reduce_max(v2c, c(2), axis=AX),
                 rd=(R_rt[2],), wr=(R_rt1,))
            b.op("act", lambda e: e.activation(tc1, v2c, AF.Exp, scale=-1.0),
                 rd=(R_rt1,), wr=(R_rt1,))
            b.op("dve", lambda e: e.tensor_scalar_add(tc1, tc1, 1.0),
                 rd=(R_rt1,), wr=(R_rt1,))
            b.op("dve", lambda e: e.reciprocal(w2c, tc1), rd=(R_rt1,), wr=(R_rt1,))
            b.op("dve", lambda e: e.tensor_scalar(w1c, w2c, -1.0, 1.0,
                                                  ALU.mult, ALU.add),
                 rd=(R_rt1,), wr=(R_rt1,))
            b.op("dve", lambda e: e.tensor_scalar(c(3), c(0), v2c, None, ALU.is_equal),
                 rd=(R_rt[0], R_rt1), wr=(R_rt[3],))
            b.op("dve", lambda e: e.tensor_scalar(c(4), c(3), w2c, None, ALU.mult),
                 rd=(R_rt[3], R_rt1), wr=(R_rt[4],))
            b.op("dve", lambda e: e.scalar_tensor_tensor(
                c(5), c(1), w1c, c(4), ALU.mult, ALU.add),
                 rd=(R_rt[1], R_rt1, R_rt[4]), wr=(R_rt[5],))
        def router_tp(tt):
            c = lambda i: rt[:, i * 8:(i + 1) * 8]
            b.mmg([(lambda: lambda e: e.transpose(
                bank[7][0:8, 0:128], c(5), id_f[:, :]))()],
                  R_bank[7], [[R_rt[5], R_const]])
            b.op("dve", (lambda tt: lambda e: e.tensor_copy(
                wst[:, tt * 128:(tt + 1) * 128], bank[7][0:8, 0:128]))(tt),
                 rd=(R_bank[7],), wr=(R_wst[tt],))

        for tt in range(4):
            t.append((lambda tt: lambda: router_tt(tt))(tt))
            t.append((lambda tt: lambda: router_tp(tt))(tt))
        t.append(lambda: b.op("sp", (lambda qc: lambda e: e.dma_start(
            wt_sp[:, qc * T:(qc + 1) * T], wst[:, :]))(qc),
            rd=tuple(R_wst), wr=(R_wtsp[qc],)))
        return t

    pend = []

    def drip(k):
        for _ in range(min(k, len(pend))):
            pend.pop(0)()

    for qc in range(NT):
        nk = 4 * (qc + 1)
        par = qc % 2
        att_s = att_b[par]
        R_att = R_attb[par]
        b.op("sp", (lambda qc: lambda e: e.dma_start(
            qt_s.ap().rearrange("p (hp t) -> p hp t", hp=8),
            q_sp[:, qc * T:(qc + 1) * T].rearrange("(hp p) t -> p hp t", p=128)))(qc),
             rd=(R_qsp[qc],), wr=(R_qts,))
        pending_norm = []
        for hp in range(8):
            def s_mm(kt, hs):
                bi = (2 * kt + hs) % 4
                fn = (lambda kt, hs, bi, hp: lambda e: e.matmul(
                    bank[bi][:, :],
                    kt_s[hs * 64:hs * 64 + 64,
                         hp * S + kt * 128:hp * S + (kt + 1) * 128],
                    qt_s[hs * 64:hs * 64 + 64, hp * T:(hp + 1) * T],
                    start=True, stop=True,
                    tile_position=(hs * 64, 0)))(kt, hs, bi, hp)
                b.mmg([fn], R_bank[bi], [[R_kts[hp], R_qts]])

            s_mm(0, 0)
            s_mm(0, 1)
            for kt in range(nk):
                if kt + 1 < nk:
                    s_mm(kt + 1, 0)
                    s_mm(kt + 1, 1)
                if kt == 1 or nk == 1:
                    for nf in pending_norm:
                        nf()
                    pending_norm = []
                if (kt == 2 and hp >= 1) or (kt == 0 and nk <= 2 and hp >= 1):
                    drip(5)
                for hs in range(2):
                    h = 2 * hp + hs
                    bi = (2 * kt + hs) % 4
                    pa = bank[4 + hs]
                    rpa = R_bank[4 + hs]
                    slot = (kt * 2 + hs) % 3
                    b.op("act", (lambda bi, slot: lambda e: e.activation(
                        pr_s[slot][:, :], bank[bi][:, :], AF.Exp, scale=0.125))(bi, slot),
                         rd=(R_bank[bi],), wr=(R_prs[slot],))
                    if kt >= 4 * qc:
                        r = kt - 4 * qc
                        b.op("dve", (lambda r, slot: lambda e: e.tensor_mul(
                            pr_s[slot][:, :], pr_s[slot][:, :],
                            mk_s[:, r * T:(r + 1) * T]))(r, slot),
                             rd=(R_mks,), wr=(R_prs[slot],))
                    fn = (lambda kt, slot, h, pa, nk: lambda e: e.matmul(
                        pa[0:65, :],
                        vh_s[:, kt * 1040 + 65 * h:kt * 1040 + 65 * h + 65],
                        pr_s[slot][:, :],
                        start=(kt == 0), stop=(kt == nk - 1)))(kt, slot, h, pa, nk)
                    w = list(rpa.wr()) if kt == 0 else []
                    w += R_vhs[kt].rd() + R_vones.rd() + R_prs[slot].rd()
                    tok = b.emit("pe", fn, w, "pe")
                    R_vhs[kt].addr(tok)
                    R_vones.addr(tok)
                    R_prs[slot].addr(tok)
                    if kt == nk - 1:
                        rpa.setw(tok)
            for hs in range(2):
                pa = bank[4 + hs]
                rpa = R_bank[4 + hs]
                rr_l = rr_s[0:1, hs * T:(hs + 1) * T]
                aslice = att_s[hs * 64:hs * 64 + 64, hp * T:(hp + 1) * T]
                areg = R_att[2 * hp + hs]
                rr = rr_b[0:1, hs * T:(hs + 1) * T]
                b.op("act", (lambda rr_l, pa: lambda e: e.activation(
                    rr_l, pa[64:65, :], AF.Ln))(rr_l, pa),
                     rd=(rpa,), wr=(R_rrs[hs], R_attf[hs]))
                b.op("dve", (lambda aslice, pa: lambda e: e.tensor_copy(
                    aslice, pa[0:64, :]))(aslice, pa),
                     rd=(rpa, R_attf[hs]), wr=(areg,))
                b.op("act", (lambda rr, rr_l: lambda e: e.activation(
                    rr, rr_l, AF.Exp, scale=-1.0))(rr, rr_l),
                     rd=(R_rrs[hs],), wr=(R_rrx,))

                def norm_fin(hs, rr, aslice, areg):
                    rbb = 6 + hs
                    p0 = hs * 64
                    b.mmg([(lambda rr, rbb, p0: lambda e: e.matmul(
                        bank[rbb][p0:p0 + 64, :], ones_b[0:1, 0:64], rr,
                        start=True, stop=True,
                        tile_position=(0, p0)))(rr, rbb, p0)],
                          R_bank[rbb], [[R_const, R_rrx]])
                    b.op("dve", (lambda rbb, p0: lambda e: e.tensor_copy(
                        rb_s[p0:p0 + 64, :], bank[rbb][p0:p0 + 64, :]))(rbb, p0),
                         rd=(R_bank[rbb],), wr=(R_rbs,))
                    b.op("dve", (lambda aslice, p0: lambda e: e.tensor_mul(
                        aslice, aslice, rb_s[p0:p0 + 64, :]))(aslice, p0),
                         rd=(R_rbs,), wr=(areg,))

                pending_norm.append(
                    (lambda hs, rr, aslice, areg: lambda: norm_fin(
                        hs, rr, aslice, areg))(hs, rr, aslice, areg))
        for nf in pending_norm:
            nf()
        drip(len(pend))
        pend = make_tail(qc, par)
    drip(len(pend))

    b.barrier()
    cur[0] = FREE0
    w1h = [salloc(f"w1h{i}", [128, 8 * 1024], BF16) for i in range(2)]
    w2h = [salloc(f"w2h{i}", [128, 8 * 1024], BF16) for i in range(2)]
    ln2c = salloc("ln2c", [128, 8 * CC], BF16)
    acc = salloc("acc", [128, 8 * CC], F32)
    h_s = [salloc(f"h_s{i}", [128, TS], BF16) for i in range(2)]
    hs_s = [salloc(f"hs_s{i}", [128, TS], BF16) for i in range(2)]
    wb_s = [salloc(f"wb_s{i}", [128, TS], BF16) for i in range(4)]
    wrow8 = salloc("wrow8", [8, CC], F32)
    wrow8b = salloc("wrow8b", [8, CC], BF16)
    oneh_b = salloc("oneh_b", [8, 8 * 128], BF16)
    b1s = salloc("b1s", [128, 128], F32)
    b2s = salloc("b2s", [8, 1024], F32)
    gms = salloc("gms", [128, 8], F32)
    bts = salloc("bts", [128, 8], F32)
    _l2f = salloc("l2f", [128, 8 * TS], F32)
    _y1f = salloc("y1f", [128, 8 * TS], F32)
    l2f = [_l2f, _l2f]
    y1f = [_y1f, _y1f]
    y2wb = [salloc(f"y2w{i}", [128, 8 * TS], F32) for i in range(4)]
    tb0 = [salloc(f"tb0{i}", [128, TS], F32) for i in range(2)]
    tb1 = [salloc(f"tb1{i}", [128, TS], F32) for i in range(2)]
    otk = [salloc(f"otk{i}", [128, 1024], F32) for i in range(2)]
    sq3 = [salloc(f"sq3_{i}", [128, TS], F32) for i in range(2)]

    R_w1h = [Reg(), Reg()]
    R_w2h = [Reg(), Reg()]
    R_ln2c = Reg()
    R_acc = [Reg() for _ in range(8)]
    R_h = [Reg(), Reg()]
    R_hs = [Reg(), Reg()]
    R_wb = [Reg() for _ in range(4)]
    R_wrow8 = Reg()
    R_wrow8b = Reg()
    R_cst3 = Reg()
    _R_l2f = Reg()
    _R_y1f = Reg()
    R_l2f = [_R_l2f, _R_l2f]
    R_y1f = [_R_y1f, _R_y1f]
    R_y2wb = [[Reg() for _ in range(8)] for _ in range(4)]
    R_tb0 = [Reg(), Reg()]
    R_tb1 = [Reg(), Reg()]
    R_otk = [Reg(), Reg()]
    R_sq3 = [Reg(), Reg()]

    b.op("sp", lambda e: e.dma_start(b1s[:, :], b1r), wr=(R_cst3,))
    b.op("pool", lambda e: e.dma_start(oneh_b[:, :], onehd), wr=(R_cst3,))
    b.op("sp", lambda e: e.dma_start(b2s[:, :], b2d), wr=(R_cst3,))
    b.op("sp", lambda e: e.dma_start(gms[:, :], gamd), wr=(R_cst3,))
    b.op("sp", lambda e: e.dma_start(bts[:, :], betd), wr=(R_cst3,))

    NP = NE * 2
    all_l2sp = [R_l2sp[i][dc] for i in range(NT) for dc in range(8)]
    all_wtsp = list(R_wtsp)

    def w_load(p, buf):
        e_, fh = p // 2, p % 2
        b.op("pool", (lambda e_, fh, buf: lambda eng: eng.dma_start(
            w1h[buf].ap().rearrange("p (dc f) -> p dc f", dc=8),
            w1[e_, :, fh * 1024:(fh + 1) * 1024].rearrange(
                "(dc p) f -> p dc f", p=128)))(e_, fh, buf),
             wr=(R_w1h[buf],))
        b.op("pool", (lambda e_, fh, buf: lambda eng: eng.dma_start(
            w2h[buf].ap().rearrange("p (fc d) -> p fc d", fc=8),
            w2[e_, fh * 1024:(fh + 1) * 1024, :].rearrange(
                "(fc p) d -> p fc d", p=128)))(e_, fh, buf),
             wr=(R_w2h[buf],))

    w_load(0, 0)
    pend3 = []
    for cc in range(NCC):
        b.op("pool", (lambda cc: lambda e: e.dma_start(
            ln2c.ap().rearrange("p (dc t) -> p dc t", dc=8),
            ln2_sp[:, cc * CC:(cc + 1) * CC].rearrange(
                "(dc p) t -> p dc t", p=128)))(cc),
             rd=tuple(all_l2sp), wr=(R_ln2c,))
        b.op("sp", (lambda cc: lambda e: e.dma_start(
            wrow8[:, :], wt_sp[:, cc * CC:(cc + 1) * CC]))(cc),
             rd=tuple(all_wtsp), wr=(R_wrow8,))
        b.op("dve", lambda e: e.tensor_copy(wrow8b[:, :], wrow8[:, :]),
             rd=(R_wrow8,), wr=(R_wrow8b,))
        for p in range(NP):
            e_, fh = p // 2, p % 2
            buf = p % 2
            pass
            if p + 1 < NP:
                w_load(p + 1, (p + 1) % 2)
            elif cc + 1 < NCC:
                w_load(0, (p + 1) % 2)
            def wb_build(ts):
                if fh != 0:
                    return
                wbs = ts
                b.mmg([(lambda e_, ts: lambda e: e.matmul(
                    bank[6][:, (ts % 2) * 256:(ts % 2) * 256 + TS],
                    oneh_b[:, e_ * 128:(e_ + 1) * 128],
                    wrow8b[:, ts * TS:(ts + 1) * TS],
                    start=True, stop=True))(e_, ts)],
                      R_bank[6], [[R_cst3, R_wrow8b]])
                b.op("act", (lambda ts, wbs: lambda e: e.activation(
                    wb_s[wbs][:, :], bank[6][:, (ts % 2) * 256:(ts % 2) * 256 + TS],
                    AF.Identity))(ts, wbs),
                     rd=(R_bank[6],), wr=(R_wb[wbs],))

            def w1_stage(ts, fc):
                hb = (ts * 8 + fc) % 2
                mms = [(lambda dc, fc, buf, ts, hb: lambda e: e.matmul(
                    bank[4 + hb][:, 0:TS],
                    w1h[buf][:, dc * 1024 + fc * 128:dc * 1024 + (fc + 1) * 128],
                    ln2c[:, dc * CC + ts * TS:dc * CC + (ts + 1) * TS],
                    start=(dc == 0), stop=(dc == 7)))(dc, fc, buf, ts, hb)
                    for dc in range(8)]
                b.mmg(mms, R_bank[4 + hb],
                      [[R_w1h[buf], R_ln2c] for _ in range(8)])
                fcg = fh * 8 + fc
                b.op("act", (lambda hb, e_, fcg: lambda e: e.activation(
                    h_s[hb][:, :], bank[4 + hb][:, 0:TS], AF.Relu,
                    bias=b1s[:, e_ * 16 + fcg:e_ * 16 + fcg + 1]))(hb, e_, fcg),
                     rd=(R_bank[4 + hb], R_cst3), wr=(R_h[hb],))
                b.op("dve", (lambda hb, ts: lambda e: e.tensor_mul(
                    hs_s[hb][:, :], h_s[hb][:, :], wb_s[ts][:, :]))(hb, ts),
                     rd=(R_h[hb], R_wb[ts]), wr=(R_hs[hb],))

            def w2_stage(ts, fc):
                hb = (ts * 8 + fc) % 2
                eo_open = eo_open_by[ts]
                is_last_piece = (p == NP - 1)
                grp_start = (fc == 0)
                grp_end = (fc == 7)
                tok = None
                for dc in range(8):
                    bi, half = dc // 2, dc % 2
                    stop = grp_end and half == 1 and not is_last_piece
                    fn = (lambda dc, fc, buf, hb, bi, half, stop, st0, ts: lambda e: e.matmul(
                        bank[bi][:, half * 256:half * 256 + TS],
                        w2h[buf][:, fc * 1024 + dc * 128:fc * 1024 + (dc + 1) * 128],
                        hs_s[hb][:, :],
                        start=st0, stop=stop))(dc, fc, buf, hb, bi, half, stop,
                                               grp_start and half == 0, ts)
                    w = []
                    if grp_start and not eo_open[bi]:
                        w += R_bank[bi].wr()
                        eo_open[bi] = True
                    w += R_w2h[buf].rd() + R_hs[hb].rd()
                    inc = "pe" if dc == 7 else None
                    tok = b.emit("pe", fn, w, inc)
                R_w2h[buf].addr(tok)
                R_hs[hb].addr(tok)
                if grp_end and not is_last_piece:
                    for bi in range(4):
                        R_bank[bi].setw(tok)

            def eo_evac(ts):
                if p == NP - 1:
                    mms = []
                    for dc in range(8):
                        bi, half = dc // 2, dc % 2
                        mms.append((lambda dc, ts, bi, half: lambda e: e.matmul(
                            bank[bi][:, half * 256:half * 256 + TS],
                            b2s[:, dc * 128:(dc + 1) * 128],
                            wrow8[:, ts * TS:(ts + 1) * TS],
                            start=False, stop=(half == 1)))(dc, ts, bi, half))
                    tok = None
                    for i, fn in enumerate(mms):
                        w = R_cst3.rd() + R_wrow8.rd()
                        tok = b.emit("pe", fn, w, "pe" if i == 7 else None)
                    R_cst3.addr(tok)
                    R_wrow8.addr(tok)
                    for bi in range(4):
                        R_bank[bi].setw(tok)
                for bi in range(4):
                    src_ap = bass.AP(bank[bi], 0, [[512, 128], [256, 2], [1, TS]])
                    dst_ap = bass.AP(acc, 2 * bi * CC + ts * TS,
                                     [[8 * CC, 128], [CC, 2], [1, TS]])
                    if p == 0:
                        fn = (lambda d, s: lambda e: e.tensor_copy(d, s))(dst_ap, src_ap)
                    else:
                        fn = (lambda d, s: lambda e: e.tensor_add(d, s, d))(dst_ap, src_ap)
                    b.op("dve", fn, rd=(R_bank[bi],),
                         wr=(R_acc[2 * bi], R_acc[2 * bi + 1]))

            eo_open_by = {ts: [False] * 4 for ts in range(NTS)}
            stages = [(ts, fc) for ts in range(NTS) for fc in range(8)]
            wb_build(0)
            w1_stage(*stages[0])
            for i in range(len(stages)):
                nts_, nfc_ = stages[i + 1] if i + 1 < len(stages) else (None, None)
                if nts_ is not None:
                    if nfc_ == 0:
                        wb_build(nts_)
                    w1_stage(nts_, nfc_)
                w2_stage(*stages[i])
                if stages[i][1] == 7:
                    eo_evac(stages[i][0])

        def ln3_a(cc, ts):
            tsb = ts % 2
            t0 = cc * CC + ts * TS
            y2w = y2wb[ts]
            R_y2w = R_y2wb[ts]
            b.op("sp", (lambda t0, tsb: lambda e: e.dma_start(
                l2f[tsb].ap().rearrange("p (dc t) -> p dc t", dc=8),
                ln2_sp[:, t0:t0 + TS].rearrange("(dc p) t -> p dc t", p=128)))(t0, tsb),
                 rd=tuple(all_l2sp), wr=(R_l2f[tsb],))
            for dc in range(8):
                b.op("dve", (lambda dc, ts, tsb, cc, y2w: lambda e: e.tensor_add(
                    y2w[:, dc * TS:(dc + 1) * TS],
                    acc[:, dc * CC + ts * TS:dc * CC + (ts + 1) * TS],
                    l2f[tsb][:, dc * TS:(dc + 1) * TS]))(dc, ts, tsb, cc, y2w),
                     rd=(R_acc[dc], R_l2f[tsb]), wr=(R_y2w[dc],))

        def ln3_b1(cc, ts):
            bk7 = bank[6 + ts % 2]
            R_bk7 = R_bank[6 + ts % 2]
            tsb = ts % 2
            t0 = cc * CC + ts * TS
            y2w = y2wb[ts]
            R_y2w = R_y2wb[ts]
            mms = [(lambda dc, y2w: lambda e: e.matmul(
                bk7[0:1, 0:TS], ones_c.ap(), y2w[:, dc * TS:(dc + 1) * TS],
                start=(dc == 0), stop=(dc == 7)))(dc, y2w) for dc in range(8)]
            b.mmg(mms, R_bk7, [[R_const, R_y2w[dc]] for dc in range(8)])
            for dc in range(8):
                sl = dc % 2
                b.op("act", (lambda dc, sl, y2w: lambda e: e.square(
                    sq3[sl][:, :], y2w[:, dc * TS:(dc + 1) * TS]))(dc, sl, y2w),
                     rd=(R_y2w[dc],), wr=(R_sq3[sl],))
                fn = (lambda dc, sl: lambda e: e.matmul(
                    bk7[32:33, 0:TS], ones_c.ap(), sq3[sl][:, :],
                    start=(dc == 0), stop=(dc == 7)))(dc, sl)
                w = (R_bk7.wr() if dc == 0 else []) + R_const.rd() + R_sq3[sl].rd()
                tok = b.emit("pe", fn, w, "pe")
                R_const.addr(tok)
                R_sq3[sl].addr(tok)
                if dc == 7:
                    R_bk7.setw(tok)

        def ln3_b2(cc, ts):
            bk7 = bank[6 + ts % 2]
            R_bk7 = R_bank[6 + ts % 2]
            tsb = ts % 2
            y2w = y2wb[ts]
            R_y2w = R_y2wb[ts]
            rM = rows[0:1, 0 * T:0 * T + TS]
            rV = rows[0:1, 1 * T:1 * T + TS]
            rA = rows[0:1, 2 * T:2 * T + TS]
            rB = rows[0:1, 3 * T:3 * T + TS]
            rT_ = rows[0:1, 4 * T:4 * T + TS]
            rL = rT_
            b.op("dve", lambda e: e.tensor_scalar_mul(rM, bk7[0:1, 0:TS], 1.0 / D),
                 rd=(R_bk7,), wr=(R_rows,))
            b.op("dve", lambda e: e.tensor_scalar_mul(rV, bk7[32:33, 0:TS], 1.0 / D),
                 rd=(R_bk7,), wr=(R_rows,))
            b.op("dve", lambda e: e.tensor_mul(rT_, rM, rM), rd=(R_rows,), wr=(R_rows,))
            b.op("dve", lambda e: e.tensor_sub(rV, rV, rT_), rd=(R_rows,), wr=(R_rows,))
            b.op("act", lambda e: e.activation(rL, rV, AF.Ln, bias=epss[0:1, 0:1]),
                 rd=(R_rows,), wr=(R_rows,))
            b.op("act", lambda e: e.activation(rA, rL, AF.Exp, scale=-0.5),
                 rd=(R_rows,), wr=(R_rows,))
            b.op("dve", lambda e: e.scalar_tensor_tensor(rB, rM, -1.0, rA,
                                                         ALU.mult, ALU.mult),
                 rd=(R_rows,), wr=(R_rows,))
            b.mmg([lambda e: e.matmul(bk7[:, 0:TS], ones_r[0:1, 0:128], rA,
                                      start=True, stop=True)],
                  R_bk7, [[R_const, R_rows]])
            b.mmg([lambda e: e.matmul(bk7[:, 256:256 + TS], ones_r[0:1, 0:128], rB,
                                      start=True, stop=True)],
                  R_bk7, [[R_const, R_rows]])
            for dc in range(8):
                sl = dc % 2
                b.op("dve", (lambda dc, sl, y2w: lambda e: e.tensor_mul(
                    tb0[sl][:, :], y2w[:, dc * TS:(dc + 1) * TS],
                    bk7[:, 0:TS]))(dc, sl, y2w),
                     rd=(R_y2w[dc], R_bk7), wr=(R_tb0[sl],))
                b.op("dve", (lambda dc, sl: lambda e: e.tensor_add(
                    tb1[sl][:, :], tb0[sl][:, :], bk7[:, 256:256 + TS]))(dc, sl),
                     rd=(R_tb0[sl], R_bk7), wr=(R_tb1[sl],))
                b.op("act", (lambda dc, sl, y2w: lambda e: e.activation(
                    y2w[:, dc * TS:(dc + 1) * TS], tb1[sl][:, :], AF.Identity,
                    scale=gms[:, dc:dc + 1], bias=bts[:, dc:dc + 1]))(dc, sl, y2w),
                     rd=(R_tb1[sl], R_cst3), wr=(R_y2w[dc],))

        def ln3_b3(cc, ts):
            tsb = 0
            t0 = cc * CC + ts * TS
            y2w = y2wb[ts]
            R_y2w = R_y2wb[ts]
            b.op("sp", (lambda t0: lambda e: e.dma_start(
                y1f[0].ap().rearrange("p (dc t) -> p dc t", dc=8),
                y1_sp[:, t0:t0 + TS].rearrange("(dc p) t -> p dc t", p=128)))(t0),
                 rd=[r for qq in range(NT) for r in R_y1sp[qq]], wr=(R_y1f[0],))
            for dc in range(8):
                b.op("dve", (lambda dc, y2w: lambda e: e.tensor_add(
                    y2w[:, dc * TS:(dc + 1) * TS], y2w[:, dc * TS:(dc + 1) * TS],
                    y1f[0][:, dc * TS:(dc + 1) * TS]))(dc, y2w),
                     rd=(R_y1f[0],), wr=(R_y2w[dc],))
            for tt2 in range(2):
                ob = tt2 % 2
                for dc in range(8):
                    tb = dc % 4
                    b.mmg([(lambda dc, tt2, tb, y2w: lambda e: e.transpose(
                        bank[tb][:, 0:128],
                        y2w[:, dc * TS + tt2 * 128:dc * TS + (tt2 + 1) * 128],
                        id_f[:, :]))(dc, tt2, tb, y2w)],
                          R_bank[tb], [[R_y2w[dc], R_const]])
                    b.op("act", (lambda dc, ob, tb: lambda e: e.activation(
                        otk[ob][:, dc * 128:(dc + 1) * 128],
                        bank[tb][:, 0:128], AF.Identity))(dc, ob, tb),
                         rd=(R_bank[tb],), wr=(R_otk[ob],))
                trow = t0 + tt2 * 128
                b.op("sp", (lambda trow, ob: lambda e: e.dma_start(
                    out_d[trow:trow + 128, :], otk[ob][:, :]))(trow, ob),
                     rd=(R_otk[ob],))

        for ts in range(NTS):
            ln3_a(cc, ts)
        for k in range(NTS + 2):
            if k < NTS:
                ln3_b1(cc, k)
            if 1 <= k <= NTS:
                ln3_b2(cc, k - 1)
            if k >= 2:
                ln3_b3(cc, k - 2)

    b.emit("sp", None, [(s, b.counts[s]) for s in DMASEMS], None)

    with nc.Block() as block:
        def runner(eng_name):
            def run(engobj):
                incsz = {s: 16 for s in DMASEMS}
                for ws, fn, inc in b.stream[eng_name]:
                    for s, v in ws:
                        engobj.wait_ge(sems[s], v)
                    if fn is not None:
                        ins = fn(engobj)
                        if inc is not None:
                            ins.then_inc(sems[inc], incsz.get(inc, 1))
            return run
        block.tensor(runner("pe"))
        block.scalar(runner("act"))
        block.vector(runner("dve"))
        block.sync(runner("sp"))
        block.gpsimd(runner("pool"))
    return nc


def _consts(S):
    pos = np.arange(S, dtype=np.float64)
    j = np.arange(DK // 2, dtype=np.float64)
    rates = 10000.0 ** (-(2.0 * j) / DK)
    ang = pos[None, :] * rates[:, None]
    cos_t = np.cos(ang)
    sin_t = np.sin(ang)
    ctab = np.zeros((128, S), np.float32)
    stab = np.zeros((128, S), np.float32)
    for r in range(128):
        lj = (r % 64) // 2
        ctab[r] = cos_t[lj]
        stab[r] = (-sin_t[lj]) if (r % 64) % 2 == 0 else sin_t[lj]
    p2 = np.zeros((128, 128), np.float32)
    for m in range(128):
        p2[m ^ 1, m] = 1.0
    ident = np.eye(128, dtype=np.float32)
    masks = np.zeros((128, 4 * T), np.float32)
    pk = np.arange(128)[:, None]
    pq = np.arange(T)[None, :]
    for r in range(4):
        masks[:, r * T:(r + 1) * T] = (pk + 128 * r <= pq).astype(np.float32)
    return ctab, stab, p2, ident, masks


_NC_CACHE = {}


def kernel(L0_attn_x, L0_attn_Wq, L0_attn_Wk, L0_attn_Wv, L0_attn_Wo,
           L0_ffn_Wg, L0_ffn_bg, L0_ffn_W1, L0_ffn_b1, L0_ffn_W2, L0_ffn_b2,
           L0_ffn_gamma, L0_ffn_beta):
    x = np.asarray(L0_attn_x, np.float32)
    B, S, _ = x.shape
    assert B == 8 and _ == D
    ctab, stab, p2, ident, masks = _consts(S)
    common = dict(
        wqt=np.ascontiguousarray(np.asarray(L0_attn_Wq, np.float32).T),
        wkt=np.ascontiguousarray(np.asarray(L0_attn_Wk, np.float32).T),
        wvt=np.ascontiguousarray(np.asarray(L0_attn_Wv, np.float32).T),
        wot=np.ascontiguousarray(np.asarray(L0_attn_Wo, np.float32).T),
        wg=np.ascontiguousarray(np.asarray(L0_ffn_Wg, np.float32)),
        bgd=np.asarray(L0_ffn_bg, np.float32).reshape(1, NE),
        w1=np.ascontiguousarray(np.asarray(L0_ffn_W1, np.float32)),
        b1r=np.ascontiguousarray(
            np.asarray(L0_ffn_b1, np.float32).reshape(NE, 16, 128)
            .transpose(2, 0, 1).reshape(128, 128)),
        w2=np.ascontiguousarray(np.asarray(L0_ffn_W2, np.float32)),
        b2d=np.ascontiguousarray(np.asarray(L0_ffn_b2, np.float32)),
        gamd=np.ascontiguousarray(
            np.asarray(L0_ffn_gamma, np.float32).reshape(8, 128).T),
        betd=np.ascontiguousarray(
            np.asarray(L0_ffn_beta, np.float32).reshape(8, 128).T),
        ctabd=ctab, stabd=stab, p2d=p2, idd=ident, maskd=masks,
        onesd=np.ones((1, 128), np.float32),
        onescd=np.ones((128, 1), np.float32),
        epsd=np.full((1, 1), EPS, np.float32),
        onehd=np.ascontiguousarray(np.kron(np.eye(8, dtype=np.float32),
                                           np.ones((1, 128), np.float32))),
    )
    key = S
    if key not in _NC_CACHE:
        _NC_CACHE[key] = build(S)
    nc = _NC_CACHE[key]
    in_maps = []
    for c in range(8):
        m = dict(common)
        m["xT"] = np.ascontiguousarray(x[c].T)
        in_maps.append(m)
    res = run_bass_kernel_spmd(nc, in_maps, core_ids=list(range(8)))
    return np.stack([r["out"] for r in res.results], axis=0)
